# version 02
# baseline (speedup 1.0000x reference)
import numpy as np
import ml_dtypes
import concourse.bass as bass
import concourse.mybir as mybir
from concourse.bass_utils import run_bass_kernel_spmd
from contextlib import ExitStack

F32 = mybir.dt.float32
BF16 = mybir.dt.bfloat16
AF = mybir.ActivationFunctionType
ALU = mybir.AluOpType
AX = mybir.AxisListType

T = 4096
D = 1024
SEG = 2048
NSEG = T // SEG
NCH = SEG // 512
ND = D // 128
NE = 32
NL = 16
B23 = 8388608.0
NEG = -1.0e30

ENTRIES = [("s", 0, 4)] + [("r", e, 4) for e in range(NE)] + [("a", l, 2) for l in range(NL)]
DEV_STAGE = 3
NEXP = len(ENTRIES)
NCHA = T // 512


class _DummyInstr:
    def then_inc(self, *a, **k):
        return self

_DI = _DummyInstr()

class _DummyEng:
    def __getattr__(self, name):
        return lambda *a, **k: _DI


class Plan:
    def __init__(self):
        self.record = True
        self.events = {}
        self.counts = {}
        self.handles = {}

    def start_pass(self, record):
        self.record = record
        self.counts = {}

    def inc(self, instr, semname, key=None, n=1):
        c = self.counts.get(semname, 0) + n
        self.counts[semname] = c
        if key is not None:
            if self.record:
                assert key not in self.events, f"dup event {key}"
                self.events[key] = (semname, c)
            else:
                assert self.events.get(key) == (semname, c), \
                    f"plan mismatch {key}: {self.events.get(key)} vs {(semname, c)}"
        if (not self.record) and (instr is not None) and not isinstance(instr, _DummyInstr):
            instr.then_inc(self.handles[semname], n)
        return instr

    def alias(self, key2, key1):
        if self.record:
            assert key1 in self.events, f"alias target missing {key1}"
            self.events[key2] = self.events[key1]

    def wait(self, eng, key):
        if key is None:
            return
        if self.record:
            return
        assert key in self.events, f"missing event {key}"
        semname, v = self.events[key]
        eng.wait_ge(self.handles[semname], v)

    def selfbar(self, eng, semname, last_instr):
        c = self.counts.get(semname, 0) + 1
        self.counts[semname] = c
        if not self.record:
            if last_instr is not None and not isinstance(last_instr, _DummyInstr):
                last_instr.then_inc(self.handles[semname], 1)
                eng.wait_ge(self.handles[semname], c)


class Ledger:

    def __init__(self):
        self.rel = {}
        self.last = {}

    def use(self, res, use_key, release_key):
        self.rel[use_key] = self.last.get(res)
        self.last[res] = release_key

    def get(self, use_key):
        return self.rel[use_key]


def plan_ffn_resources(ph):
    L = Ledger()
    for c in range(NCH):
        for d in range(ND):
            L.use("xb%d" % (d % 2), ("xb", ph, c, d), ("SXmm", ph, c, d))
            L.use("xq", ("xq", ph, c, d), ("SQmm", ph, c, d))
        L.use("pG0", ("statA", ph, c), ("STEV", ph, c))
        L.use("pG1", ("statB", ph, c), ("STEV", ph, c))
        L.use("pU0", ("repm", ph, c), ("LNB", ph, c))
        L.use("pU1", ("repi", ph, c), ("LNB", ph, c))
        for q in range(4):
            L.use("pD0", ("log", ph, c, q), ("SIG", ph, c, q))
    def use_gu(i, c, nic):
        for ic in range(nic):
            L.use("pG%d" % (ic % 2), ("g", ph, i, c, ic), ("SIL", ph, i, c, ic))
            L.use("pU%d" % (ic % 2), ("u", ph, i, c, ic), ("VA", ph, i, c, ic))
        L.use("ssil", ("ssil", ph, i, c), ("VA", ph, i, c, nic - 1))
        L.use("h%d" % (c % 2), ("hbuf", ph, i, c), ("DD", ph, i, c, 7))

    def use_down(i, c):
        for dc in range(8):
            L.use("pD%d" % (dc % 2), ("d", ph, i, c, dc), ("EV", ph, i, c, dc))

    nic0 = ENTRIES[0][2]
    for c in range(NCH):
        use_gu(0, c, nic0)
    use_down(0, 0)
    use_down(0, 1)
    for r in range(8):
        L.use("pD1" if r % 2 == 0 else "pCW1", ("trr", ph, r), ("TRE", ph, r))
    pend = [(0, 2), (0, 3)]
    for i in range(1, NEXP):
        kind, _, nic = ENTRIES[i]
        for c in range(NCH):
            if kind != "s":
                L.use("pCW%d" % ((i * NCH + c) % 2), ("cwrep", ph, i, c), ("CWE", ph, i, c))
                L.use("cwsb", ("cwsb", ph, i, c), ("VB", ph, i, c))
            use_gu(i, c, nic)
            use_down(*pend.pop(0))
            pend.append((i, c))
    while pend:
        use_down(*pend.pop(0))
    for i in range(NEXP):
        L.use("wbuf%d" % (i % 2), ("wbuf", ph, i), ("WF", ph, i))
    return L


def plan_att_resources(ph):
    L = Ledger()
    for c in range(NCHA):
        for d in range(ND):
            L.use("axb%d" % (d % 2), ("axb", ph, c, d), ("aSXmm", ph, c, d))
            L.use("axq", ("axq", ph, c, d), ("aSQmm", ph, c, d))
        L.use("pA0", ("astatA", ph, c), ("aSTEV", ph, c))
        L.use("pA1", ("astatB", ph, c), ("aSTEV", ph, c))
        L.use("pA0", ("arepm", ph, c), ("LNA", ph, c))
        L.use("pA1", ("arepi", ph, c), ("LNA", ph, c))
        L.use("ln2", ("ln2", ph, c), ("VMM", ph, c, 3, 1))
        L.use("pA2", ("lat", ph, c), ("LATE", ph, c))
        for tt in range(4):
            for dn in range(2):
                L.use("pA%d" % (3 + (tt * 2 + dn) % 2), ("vmm", ph, c, tt, dn),
                      ("VE", ph, c, tt, dn))
    for m in range(NCHA):
        nkc = m + 1
        nkt = 4 * (m + 1)
        for qt in range(4):
            for kc in range(nkc):
                L.use("pA%d" % (kc % 2), ("sc", ph, m, qt, kc), ("SCE", ph, m, qt, kc))
            ngr = (nkt + 3) // 4
            for gr in range(ngr):
                L.use("pA%d" % (2 + gr % 2), ("tr", ph, m, qt, gr), ("TRE2", ph, m, qt, gr))
            L.use("scrow", ("scrow", ph, m, qt), ("AEXP", ph, m, qt))
            L.use("attbf", ("attbf", ph, m, qt), ("TRLAST", ph, m, qt))
        L.use("attT", ("attT", ph, m), ("AAV", ph, m, 7))
        for dc in range(8):
            L.use("pA%d" % dc, ("av", ph, m, dc), ("AYE", ph, m, dc))
        L.use("ysb", ("ysb", ph, m), ("AWO", ph, m, 7))
        for dc2 in range(8):
            L.use("pA%d" % (dc2 % 2), ("wo", ph, m, dc2), ("ACP", ph, m, dc2))
            L.use("osb%d" % (dc2 % 2), ("osb", ph, m, dc2), ("AX2", ph, m, dc2))
        L.use("x1m", ("x1m", ph, m), ("X2S", ph, m))
    return L


def pe_ffn_order():
    seq = []
    pend = []
    for i in range(NEXP):
        for c in range(NCH):
            seq.append(("gu", i, c))
            pend.append((i, c))
            if len(pend) > 1:
                seq.append(("down",) + pend.pop(0))
    while pend:
        seq.append(("down",) + pend.pop(0))
    return seq


def build_program(phases=("f1", "att", "f2"), dev_expose=()):
    nc = bass.Bass("TRN2", target_bir_lowering=False)
    P = Plan()

    ap = {}

    def din(name, shape):
        ap[name] = nc.dram_tensor(name, list(shape), F32, kind="ExternalInput").ap()

    din("xT", (D, T))
    din("wbT", (D, 64)); din("wvT", (D, D)); din("woT", (D, D))
    din("bvrep", (128, D)); din("bopp", (128, 8))
    din("gwT", (D, 32)); din("gbw", (128, 512))
    din("wg", (NE, D, 512)); din("wu", (NE, D, 512)); din("wd", (NE, 512, D))
    din("swg", (D, 512)); din("swu", (D, 512)); din("swd", (512, D))
    din("awg", (NL, D, 256)); din("awu", (NL, D, 256)); din("awd", (NL, 256, D))
    din("ones128", (128, 128)); din("negones", (1, 128)); din("ident", (128, 128))
    din("ohsel32", (128, 32 * 128))
    din("iota32w", (128, 512)); din("iota8w", (128, 512)); din("iota4w", (128, 64))
    din("mask128", (128, 128))

    x1 = nc.dram_tensor("x1", [D, T], F32,
                        kind="ExternalOutput" if "x1" in dev_expose else "Internal").ap()
    x2 = nc.dram_tensor("x2", [D, T], F32,
                        kind="ExternalOutput" if "x2" in dev_expose else "Internal").ap()
    outT = nc.dram_tensor("outT", [D, T], F32, kind="ExternalOutput").ap()

    es = ExitStack()

    def sb(name, shape, dt):
        return es.enter_context(nc.sbuf_tensor(name, list(shape), dt))

    ones_bf = sb("ones_bf", (128, 128), BF16)
    negones_bf = sb("negones_bf", (1, 128), BF16)
    ident_f = sb("ident_f", (128, 128), F32)
    ident_bf = sb("ident_bf", (128, 128), BF16)
    gwT_sb = sb("gwT_sb", (128, ND, 32), BF16)
    banks = [es.enter_context(nc.psum_tensor("bank%d" % p, [128, 512], F32))
             for p in range(8)]

    semnames = ["WD0", "WD1", "CWD", "AWD", "ADC", "XD", "XS", "CWR", "CWE",
                "GG", "UU", "SIL", "VA", "VB", "DD", "EV", "VS", "SS", "TS",
                "PB", "RT", "AD"]
    for s_ in semnames:
        P.handles[s_] = es.enter_context(nc.semaphore(s_))

    bar = {"target": 0}

    def bwait(eng):
        if bar["target"] > 0 and not P.record:
            eng.wait_ge(P.handles["PB"], bar["target"])

    def binc(eng):
        P.counts["PB"] = P.counts.get("PB", 0) + 1
        if not P.record:
            eng.sem_inc(P.handles["PB"], 1)

    def dispatch(name, em_sync, em_gpsimd, em_tensor, em_scalar, em_vector):
        if P.record:
            em_sync(_DummyEng())
            em_gpsimd(_DummyEng())
            em_tensor(_DummyEng())
            em_scalar(_DummyEng())
            em_vector(_DummyEng())
        else:
            with nc.Block(name) as block:
                block.sync(em_sync)
                block.gpsimd(em_gpsimd)
                block.tensor(em_tensor)
                block.scalar(em_scalar)
                block.vector(em_vector)

    def ffn_seg(ph, s_, src_dram, dst_dram, B, first_seg_of_program):
        RL = plan_ffn_resources(ph)
        B["cwtm"] = B["uscr"][:, 0, :]
        B["acwtm"] = B["uscr"][:, 2, 0:256]

        tok0 = s_ * SEG
        x_src = src_dram.rearrange("(c p) t -> p c t", p=128)[:, :, tok0:tok0 + SEG]
        x_dst = dst_dram.rearrange("(c p) t -> p c t", p=128)[:, :, tok0:tok0 + SEG]

        if True:
            def em_sync(sy):
                bwait(sy)
                i1 = sy.dma_start(B["x_sb"][:], x_src)
                P.inc(i1, "XD", ("XD", ph), 16)
                if DEV_STAGE >= 3:
                    P.wait(sy, ("EV", ph, NEXP - 1, NCH - 1, 7))
                elif DEV_STAGE == 2:
                    P.wait(sy, ("RTDONE", ph))
                elif DEV_STAGE == 1:
                    P.wait(sy, ("LNB", ph, NCH - 1))
                else:
                    if not P.record:
                        sy.wait_ge(P.handles["XD"], P.events[("XD", ph)][1])
                i2 = sy.dma_start(x_dst, B["x_sb"][:])
                P.inc(i2, "XS", ("XS", ph), 16)
                if not P.record:
                    sy.wait_ge(P.handles["XS"], P.events[("XS", ph)][1])
                binc(sy)

            def em_gpsimd(g):
                bwait(g)
                if first_seg_of_program:
                    P.inc(g.dma_start(ones_bf[:], ap["ones128"]), "CWD", None, 16)
                    P.inc(g.dma_start(negones_bf[:], ap["negones"]), "CWD", None, 16)
                    P.inc(g.dma_start(ident_f[:], ap["ident"]), "CWD", None, 16)
                    P.inc(g.dma_start(ident_bf[:], ap["ident"]), "CWD", None, 16)
                    P.inc(g.dma_start(gwT_sb[:], ap["gwT"].rearrange("(c p) e -> p c e", p=128)), "CWD", None, 16)
                if s_ == 0:
                    P.inc(g.dma_start(B["ohsel32"][:], ap["ohsel32"]), "CWD", None, 16)
                    P.inc(g.dma_start(B["iota32w"][:], ap["iota32w"]), "CWD", None, 16)
                    P.inc(g.dma_start(B["iota8w"][:], ap["iota8w"]), "CWD", None, 16)
                    P.inc(g.dma_start(B["iota4w"][:], ap["iota4w"]), "CWD", None, 16)
                    i0 = g.dma_start(B["gbw"][:], ap["gbw"])
                    P.inc(i0, "CWD", ("CONSTS", ph), 16)
                for i, (kind, e, nic) in enumerate(ENTRIES):
                    if DEV_STAGE < 3:
                        break
                    p = i % 2
                    P.wait(g, RL.get(("wbuf", ph, i)))
                    if kind == "r":
                        sg = ap["wg"][e].rearrange("(c p) i -> p c i", p=128)
                        su = ap["wu"][e].rearrange("(c p) i -> p c i", p=128)
                        sd = ap["wd"][e].rearrange("(c p) i -> p c i", p=128)
                        dg, du, dd = B["wgb"][p][:], B["wub"][p][:], B["wdb"][p][:]
                    elif kind == "a":
                        sg = ap["awg"][e].rearrange("(c p) i -> p c i", p=128)
                        su = ap["awu"][e].rearrange("(c p) i -> p c i", p=128)
                        sd = ap["awd"][e].rearrange("(c p) i -> p c i", p=128)
                        dg = B["wgb"][p][:, :, 0:256]
                        du = B["wub"][p][:, :, 0:256]
                        dd = B["wdb"][p][:, 0:2, :]
                    else:
                        sg = ap["swg"].rearrange("(c p) i -> p c i", p=128)
                        su = ap["swu"].rearrange("(c p) i -> p c i", p=128)
                        sd = ap["swd"].rearrange("(c p) i -> p c i", p=128)
                        dg, du, dd = B["wgb"][p][:], B["wub"][p][:], B["wdb"][p][:]
                    wsem = "WD%d" % (i % 2)
                    P.inc(g.dma_start(dg, sg), wsem, None, 16)
                    P.inc(g.dma_start(du, su), wsem, None, 16)
                    P.inc(g.dma_start(dd, sd), wsem, ("WDall", ph, i), 16)
                binc(g)

            def em_tensor(t):
                bwait(t)
                if s_ == 0:
                    P.wait(t, ("CONSTS", ph))
                for c in range(NCH):
                    if DEV_STAGE < 1:
                        break
                    P.wait(t, RL.get(("statA", ph, c)))
                    P.wait(t, RL.get(("statB", ph, c)))
                    for d in range(ND):
                        P.wait(t, ("XB", ph, c, d))
                        i1 = t.matmul(B["pG"][0][0:1, :], ones_bf[:, 0:1],
                                      B["xb"][:, d % 2, :], start=(d == 0), stop=(d == ND - 1))
                        P.inc(i1, "TS", ("SXmm", ph, c, d))
                        P.wait(t, ("XQ", ph, c, d))
                        i2 = t.matmul(B["pG"][1][0:1, :], ones_bf[:, 0:1],
                                      B["xq"][:, 0, :], start=(d == 0), stop=(d == ND - 1))
                        P.inc(i2, "TS", ("SQmm", ph, c, d))
                    P.alias(("STAT", ph, c), ("SQmm", ph, c, ND - 1))
                    P.wait(t, ("HILO", ph, c))
                    P.wait(t, RL.get(("repm", ph, c)))
                    P.wait(t, RL.get(("repi", ph, c)))
                    t.matmul(B["pU"][0][:], negones_bf[:], B["hilo"][0:1, 0, :], start=True, stop=False)
                    t.matmul(B["pU"][0][:], negones_bf[:], B["hilo"][0:1, 1, :], start=False, stop=True)
                    t.matmul(B["pU"][1][:], ones_bf[0:1, :], B["hilo"][0:1, 2, :], start=True, stop=False)
                    i3 = t.matmul(B["pU"][1][:], ones_bf[0:1, :], B["hilo"][0:1, 3, :], start=False, stop=True)
                    P.inc(i3, "TS", ("REP", ph, c))
                    P.wait(t, ("LNB", ph, c))
                    for q in range(4):
                        if DEV_STAGE < 2:
                            break
                        P.wait(t, RL.get(("log", ph, c, q)))
                        for d in range(ND):
                            i4 = t.matmul(B["pD"][0][:, q * 32:(q + 1) * 32],
                                          B["ln_sb"][:, d, c * 512 + q * 128: c * 512 + (q + 1) * 128],
                                          gwT_sb[:, d, :], start=(d == 0), stop=(d == ND - 1))
                        P.inc(i4, "TS", ("LOG", ph, c, q))
                def t_gu(i, c, prev):
                    kind, e, nic = ENTRIES[i]
                    p = i % 2
                    if prev is not None:
                        pi, pc = prev
                        pn = ENTRIES[pi][2]
                        P.wait(t, ("SIL", ph, pi, pc, pn - 1))
                        P.wait(t, ("VA", ph, pi, pc, pn - 1))
                    if c == 0:
                        P.wait(t, ("WDall", ph, i))
                    for ic in range(nic):
                        if prev is None:
                            P.wait(t, RL.get(("g", ph, i, c, ic)))
                            P.wait(t, RL.get(("u", ph, i, c, ic)))
                        elif ic >= 2:
                            P.wait(t, ("SIL", ph, i, c, ic - 2))
                            P.wait(t, ("VA", ph, i, c, ic - 2))
                        for d in range(ND):
                            i7 = t.matmul(B["pG"][ic % 2][:],
                                          B["wgb"][p][:, d, ic * 128:(ic + 1) * 128],
                                          B["ln_sb"][:, d, c * 512:(c + 1) * 512],
                                          start=(d == 0), stop=(d == ND - 1))
                        P.inc(i7, "GG", ("GG", ph, i, c, ic))
                        for d in range(ND):
                            i8 = t.matmul(B["pU"][ic % 2][:],
                                          B["wub"][p][:, d, ic * 128:(ic + 1) * 128],
                                          B["ln_sb"][:, d, c * 512:(c + 1) * 512],
                                          start=(d == 0), stop=(d == ND - 1))
                        P.inc(i8, "UU", ("UU", ph, i, c, ic))

                def t_cwrep(i, c):
                    kind, e, nic = ENTRIES[i]
                    if kind == "s":
                        return
                    bank = B["pCW2"][(i * NCH + c) % 2]
                    P.wait(t, RL.get(("cwrep", ph, i, c)))
                    if kind == "r":
                        i6 = t.matmul(bank[:], B["ohsel32"][0:64, e * 128:(e + 1) * 128],
                                      B["cwT"][0:64, c * 512:(c + 1) * 512], start=True, stop=True)
                    else:
                        i6 = t.matmul(bank[:], B["ohsel32"][64:80, e * 128:(e + 1) * 128],
                                      B["cwT"][64:80, c * 512:(c + 1) * 512], start=True, stop=True)
                    P.inc(i6, "CWR", ("CWR", ph, i, c))

                def t_down(i, c, prevd):
                    kind, e, nic = ENTRIES[i]
                    p = i % 2
                    P.wait(t, ("VB", ph, i, c))
                    if prevd is not None:
                        P.wait(t, ("EV", ph, prevd[0], prevd[1], 7))
                    for dc in range(8):
                        if prevd is None:
                            P.wait(t, RL.get(("d", ph, i, c, dc)))
                        elif dc >= 2:
                            P.wait(t, ("EV", ph, i, c, dc - 2))
                        for ic in range(nic):
                            i9 = t.matmul(B["pD"][dc % 2][:],
                                          B["wdb"][p][:, ic, dc * 128:(dc + 1) * 128],
                                          B["h"][:, c % 2, ic, :],
                                          start=(ic == 0), stop=(ic == nic - 1))
                        P.inc(i9, "DD", ("DD", ph, i, c, dc))
                    if c == NCH - 1:
                        P.alias(("WF", ph, i), ("DD", ph, i, c, 7))

                if DEV_STAGE >= 3:
                    prev_gu = None
                    prevd = None
                    for c in range(NCH):
                        t_gu(0, c, prev_gu)
                        prev_gu = (0, c)
                        if c >= 2:
                            t_down(0, c - 2, prevd)
                            prevd = (0, c - 2)
                if DEV_STAGE >= 2:
                    P.wait(t, ("RT_cw", ph))
                    for r in range(8):
                        P.wait(t, RL.get(("trr", ph, r)))
                        bank = B["pD"][1] if r % 2 == 0 else B["pCW2"][1]
                        for k in range(4):
                            sl = r * 4 + k
                            if r < 4:
                                i5 = t.transpose(bank[0:32, k * 128:(k + 1) * 128],
                                                 B["cwtm"][:, sl * 32:(sl + 1) * 32], ident_f[:])
                            else:
                                i5 = t.transpose(bank[0:16, k * 128:(k + 1) * 128],
                                                 B["acwtm"][:, (sl - 16) * 16:(sl - 15) * 16], ident_f[:])
                        P.inc(i5, "TS", ("TRR", ph, r))
                if DEV_STAGE >= 3:
                    pend = [(0, 2), (0, 3)]
                    for i in range(1, NEXP):
                        for c in range(NCH):
                            t_cwrep(i, c)
                            t_gu(i, c, prev_gu)
                            prev_gu = (i, c)
                            jj = pend.pop(0)
                            t_down(*jj, prevd)
                            prevd = jj
                            pend.append((i, c))
                    while pend:
                        jj = pend.pop(0)
                        t_down(*jj, prevd)
                        prevd = jj
                binc(t)

            def em_scalar(sc_):
                bwait(sc_)
                P.wait(sc_, ("XD", ph))
                for c in range(NCH):
                    if DEV_STAGE < 1:
                        break
                    for d in range(ND):
                        P.wait(sc_, RL.get(("xq", ph, c, d)))
                        i1 = sc_.activation(B["xq"][:, 0, :],
                                            B["x_sb"][:, d, c * 512:(c + 1) * 512], AF.Square)
                        P.inc(i1, "SS", ("XQ", ph, c, d))
                    P.wait(sc_, ("VAR", ph, c))
                    i2 = sc_.activation(B["vvec"][:], B["vvec"][:], AF.Ln)
                    P.selfbar(sc_, "SS", i2)
                    i3 = sc_.mul(B["vvec"][:], B["vvec"][:], -0.5)
                    P.selfbar(sc_, "SS", i3)
                    i4 = sc_.activation(B["vvec"][:], B["vvec"][:], AF.Exp)
                    P.inc(i4, "SS", ("INV", ph, c))
                    for q in range(4):
                        if DEV_STAGE < 2:
                            break
                        P.wait(sc_, ("LOG", ph, c, q))
                        i5 = sc_.activation(B["sc"][:, (c * 4 + q) * 32:(c * 4 + q + 1) * 32],
                                            B["pD"][0][:, q * 32:(q + 1) * 32], AF.Sigmoid)
                        P.inc(i5, "SS", ("SIG", ph, c, q))
                def sc_cwevac(i, c):
                    kind, e, nic = ENTRIES[i]
                    if kind == "s":
                        return
                    P.wait(sc_, ("CWR", ph, i, c))
                    P.wait(sc_, RL.get(("cwsb", ph, i, c)))
                    i6 = sc_.copy(B["cwrep"][:, 0, :], B["pCW2"][(i * NCH + c) % 2][:])
                    P.inc(i6, "CWE", ("CWE", ph, i, c))

                def sc_silu(i, c):
                    kind, e, nic = ENTRIES[i]
                    P.wait(sc_, RL.get(("ssil", ph, i, c)))
                    for ic in range(nic):
                        P.wait(sc_, ("GG", ph, i, c, ic))
                        i7 = sc_.activation(B["ssil"][:, ic, :], B["pG"][ic % 2][:], AF.Silu)
                        P.inc(i7, "SIL", ("SIL", ph, i, c, ic))

                if DEV_STAGE >= 3:
                    for c in range(NCH):
                        sc_silu(0, c)
                    for c in range(NCH):
                        sc_cwevac(0, c)
                    for i in range(1, NEXP):
                        for c in range(NCH):
                            sc_cwevac(i, c)
                            sc_silu(i, c)
                binc(sc_)

            def em_vector(v):
                bwait(v)
                P.wait(v, ("XD", ph))
                for c in range(NCH):
                    if DEV_STAGE < 1:
                        break
                    for d in range(ND):
                        P.wait(v, RL.get(("xb", ph, c, d)))
                        i1 = v.tensor_copy(B["xb"][:, d % 2, :],
                                           B["x_sb"][:, d, c * 512:(c + 1) * 512])
                        P.inc(i1, "VS", ("XB", ph, c, d))
                    P.wait(v, ("STAT", ph, c))
                    v.tensor_copy(B["sx"][0:1, 0, :], B["pG"][0][0:1, :])
                    i3 = v.tensor_copy(B["sx"][0:1, 1, :], B["pG"][1][0:1, :])
                    P.inc(i3, "VS", ("STEV", ph, c))
                    P.wait(v, ("STEV", ph, c))
                    i4 = v.tensor_scalar(B["mvec"][:], B["sx"][0:1, 0, :], 1.0 / D, None, ALU.mult)
                    P.selfbar(v, "VS", i4)
                    i5 = v.tensor_mul(B["vvec"][:], B["mvec"][:], B["mvec"][:])
                    P.selfbar(v, "VS", i5)
                    i6 = v.scalar_tensor_tensor(B["vvec"][:], B["sx"][0:1, 1, :], 1.0 / D,
                                                B["vvec"][:], ALU.mult, ALU.subtract)
                    P.selfbar(v, "VS", i6)
                    i7 = v.tensor_scalar(B["vvec"][:], B["vvec"][:], 1.0, 1e-5, ALU.mult, ALU.add)
                    P.inc(i7, "VS", ("VAR", ph, c))
                    i8 = v.tensor_copy(B["hilo"][0:1, 0, :], B["mvec"][:])
                    P.selfbar(v, "VS", i8)
                    i9 = v.tensor_sub(B["sx"][0:1, 0, :], B["mvec"][:], B["hilo"][0:1, 0, :])
                    P.selfbar(v, "VS", i9)
                    v.tensor_copy(B["hilo"][0:1, 1, :], B["sx"][0:1, 0, :])
                    P.wait(v, ("INV", ph, c))
                    i11 = v.tensor_copy(B["hilo"][0:1, 2, :], B["vvec"][:])
                    P.selfbar(v, "VS", i11)
                    i12 = v.tensor_sub(B["sx"][0:1, 1, :], B["vvec"][:], B["hilo"][0:1, 2, :])
                    P.selfbar(v, "VS", i12)
                    i13 = v.tensor_copy(B["hilo"][0:1, 3, :], B["sx"][0:1, 1, :])
                    P.inc(i13, "VS", ("HILO", ph, c))
                    P.wait(v, ("REP", ph, c))
                    for dh in range(2):
                        for d4 in range(4):
                            d = dh * 4 + d4
                            i14 = v.tensor_add(B["uscr"][:, d4, :],
                                               B["x_sb"][:, d, c * 512:(c + 1) * 512], B["pU"][0][:])
                        P.selfbar(v, "VS", i14)
                        for d4 in range(4):
                            d = dh * 4 + d4
                            i15 = v.tensor_mul(B["ln_sb"][:, d, c * 512:(c + 1) * 512],
                                               B["uscr"][:, d4, :], B["pU"][1][:])
                        if dh == 0:
                            P.selfbar(v, "VS", i15)
                    P.inc(i15, "VS", ("LNB", ph, c))
                if DEV_STAGE >= 2:
                  for q in range(4):
                    P.wait(v, ("SIG", ph, NCH - 1, q))
                if DEV_STAGE >= 2:
                    scv = B["sc"][:]
                    scb = B["uscr"][:, 0, :]
                    msc = B["uscr"][:, 1, :]
                    tmp1 = B["uscr"][:, 2, :]
                    tmp2 = B["uscr"][:, 3, :]
                    gt = B["gt"]
                    st = B["st"]
                    selm = gt[:].rearrange("p a c -> p (a c)")

                    def v3(apx, k):
                        return apx.rearrange("p (a c) -> p a c", c=k)

                    def bc(apx, g_, k):
                        return apx.to_broadcast([128, g_, k])

                    def VB_(i):
                        P.selfbar(v, "VS", i)

                    i = v.tensor_add(scb, scv, B["gbw"][:]); VB_(i)
                    i = v.reduce_max(gt[:, 0, :], v3(scb, 8), axis=AX.X); VB_(i)
                    i = v.tensor_tensor(v3(tmp1, 8), v3(scb, 8), bc(gt[:, 0, :], 64, 8), ALU.is_equal); VB_(i)
                    i = v.scalar_tensor_tensor(tmp2, tmp1, -B23, B["iota8w"][:], ALU.mult, ALU.add); VB_(i)
                    i = v.tensor_reduce(gt[:, 1, :], v3(tmp2, 8), axis=AX.X, op=ALU.min); VB_(i)
                    i = v.tensor_scalar(gt[:, 1, :], gt[:, 1, :], 1.0, B23, ALU.mult, ALU.add); VB_(i)
                    i = v.tensor_tensor(v3(tmp1, 8), v3(B["iota8w"][:], 8), bc(gt[:, 1, :], 64, 8), ALU.is_equal); VB_(i)
                    i = v.scalar_tensor_tensor(tmp2, tmp1, NEG, scb, ALU.mult, ALU.add); VB_(i)
                    i = v.reduce_max(gt[:, 2, :], v3(tmp2, 8), axis=AX.X); VB_(i)
                    i = v.tensor_add(gt[:, 3, :], gt[:, 0, :], gt[:, 2, :]); VB_(i)
                    gs = gt[:, 3, :]
                    i = v.reduce_max(st[:, 0, :], v3(gs, 4), axis=AX.X); VB_(i)
                    i = v.tensor_tensor(v3(gt[:, 4, :], 4), v3(gs, 4), bc(st[:, 0, :], 16, 4), ALU.is_equal); VB_(i)
                    i = v.scalar_tensor_tensor(gt[:, 5, :], gt[:, 4, :], -B23, B["iota4w"][:], ALU.mult, ALU.add); VB_(i)
                    i = v.tensor_reduce(st[:, 1, :], v3(gt[:, 5, :], 4), axis=AX.X, op=ALU.min); VB_(i)
                    i = v.tensor_scalar(st[:, 1, :], st[:, 1, :], 1.0, B23, ALU.mult, ALU.add); VB_(i)
                    i = v.tensor_tensor(v3(gt[:, 4, :], 4), v3(B["iota4w"][:], 4), bc(st[:, 1, :], 16, 4), ALU.is_equal); VB_(i)
                    i = v.scalar_tensor_tensor(gt[:, 5, :], gt[:, 4, :], NEG, gs, ALU.mult, ALU.add); VB_(i)
                    i = v.reduce_max(st[:, 0, :], v3(gt[:, 5, :], 4), axis=AX.X); VB_(i)
                    i = v.tensor_tensor(v3(gt[:, 6, :], 4), v3(gt[:, 5, :], 4), bc(st[:, 0, :], 16, 4), ALU.is_equal); VB_(i)
                    i = v.scalar_tensor_tensor(gt[:, 5, :], gt[:, 6, :], -B23, B["iota4w"][:], ALU.mult, ALU.add); VB_(i)
                    i = v.tensor_reduce(st[:, 1, :], v3(gt[:, 5, :], 4), axis=AX.X, op=ALU.min); VB_(i)
                    i = v.tensor_scalar(st[:, 1, :], st[:, 1, :], 1.0, B23, ALU.mult, ALU.add); VB_(i)
                    i = v.tensor_tensor(v3(gt[:, 6, :], 4), v3(B["iota4w"][:], 4), bc(st[:, 1, :], 16, 4), ALU.is_equal); VB_(i)
                    i = v.tensor_add(gt[:, 7, :], gt[:, 4, :], gt[:, 6, :]); VB_(i)
                    i = v.tensor_scalar(gt[:, 6, :], gt[:, 7, :], -NEG, NEG, ALU.mult, ALU.add); VB_(i)
                    i = v.tensor_tensor(v3(tmp1, 8), v3(scb, 8), bc(gt[:, 7, :], 64, 8), ALU.mult); VB_(i)
                    i = v.tensor_tensor(v3(msc, 8), v3(tmp1, 8), bc(gt[:, 6, :], 64, 8), ALU.add); VB_(i)
                    i = v.memset(selm, 0.0); VB_(i)
                    for k in range(6):
                        i = v.reduce_max(st[:, 0, :], v3(msc, 32), axis=AX.X); VB_(i)
                        i = v.tensor_tensor(v3(tmp1, 32), v3(msc, 32), bc(st[:, 0, :], 16, 32), ALU.is_equal); VB_(i)
                        i = v.scalar_tensor_tensor(tmp2, tmp1, -B23, B["iota32w"][:], ALU.mult, ALU.add); VB_(i)
                        i = v.tensor_reduce(st[:, 1, :], v3(tmp2, 32), axis=AX.X, op=ALU.min); VB_(i)
                        i = v.tensor_scalar(st[:, 1, :], st[:, 1, :], 1.0, B23, ALU.mult, ALU.add); VB_(i)
                        if k == 0:
                            i = v.tensor_copy(B["idx0"][:], st[:, 1, :]); VB_(i)
                        i = v.tensor_tensor(v3(tmp1, 32), v3(B["iota32w"][:], 32), bc(st[:, 1, :], 16, 32), ALU.is_equal); VB_(i)
                        i = v.tensor_add(selm, selm, tmp1); VB_(i)
                        i = v.scalar_tensor_tensor(msc, tmp1, NEG, msc, ALU.mult, ALU.add); VB_(i)
                    i = v.tensor_mul(tmp1, selm, scv); VB_(i)
                    i = v.reduce_sum(st[:, 0, :], v3(tmp1, 32), axis=AX.X); VB_(i)
                    i = v.tensor_scalar(st[:, 1, :], st[:, 0, :], 1.0, 1e-20, ALU.mult, ALU.add); VB_(i)
                    i = v.reciprocal(st[:, 0, :], st[:, 1, :]); VB_(i)
                    i = v.tensor_scalar(st[:, 1, :], st[:, 0, :], 0.1, None, ALU.mult); VB_(i)
                    i = v.tensor_tensor(v3(B["cwtm"][:], 32), v3(tmp1, 32), bc(st[:, 1, :], 16, 32), ALU.mult); VB_(i)
                    i = v.tensor_tensor(v3(tmp2, 32), v3(B["iota32w"][:], 32), bc(B["idx0"][:], 16, 32), ALU.is_equal); VB_(i)
                    t2v = tmp2.rearrange("p (a l two) -> p a l two", l=16, two=2)
                    i = v.tensor_tensor(B["acwtm"][:].rearrange("p (a l) -> p a l", l=16),
                                        t2v[:, :, :, 0], t2v[:, :, :, 1], ALU.add)
                    P.inc(i, "RT", ("RT_cw", ph))
                    P.wait(v, ("RT_cw", ph))
                pend = []
                prev_ic = [None]

                def v_stageA(i_e, c):
                    kind, e, nic = ENTRIES[i_e]
                    if prev_ic[0] is not None:
                        P.wait(v, ("VB", ph) + prev_ic[0])
                    prev_ic[0] = (i_e, c)
                    for ic in range(nic):
                        P.wait(v, ("UU", ph, i_e, c, ic))
                        P.wait(v, ("SIL", ph, i_e, c, ic))
                        ia = v.tensor_mul(B["h1"][:, ic, :], B["pU"][ic % 2][:],
                                          B["ssil"][:, ic, :])
                        P.inc(ia, "VA", ("VA", ph, i_e, c, ic))

                def v_stageB(i_e, c):
                    kind, e, nic = ENTRIES[i_e]
                    P.wait(v, ("VA", ph, i_e, c, nic - 1))
                    P.wait(v, RL.get(("hbuf", ph, i_e, c)))
                    if kind != "s":
                        P.wait(v, ("CWE", ph, i_e, c))
                    for ic in range(nic):
                        if kind == "a":
                            ib = v.scalar_tensor_tensor(B["h"][:, c % 2, ic, :],
                                                        B["h1"][:, ic, :], 0.05,
                                                        B["cwrep"][:, 0, :], ALU.mult, ALU.mult)
                        elif kind == "r":
                            ib = v.tensor_mul(B["h"][:, c % 2, ic, :], B["h1"][:, ic, :],
                                              B["cwrep"][:, 0, :])
                        else:
                            ib = v.tensor_scalar(B["h"][:, c % 2, ic, :], B["h1"][:, ic, :],
                                                 0.5, None, ALU.mult)
                    P.inc(ib, "VB", ("VB", ph, i_e, c))

                def evac(j_e, jc):
                    for dc in range(8):
                        P.wait(v, ("DD", ph, j_e, jc, dc))
                        if j_e > 0:
                            P.wait(v, ("EV", ph, j_e - 1, jc, dc))
                        ie = v.tensor_add(B["x_sb"][:, dc, jc * 512:(jc + 1) * 512],
                                          B["x_sb"][:, dc, jc * 512:(jc + 1) * 512],
                                          B["pD"][dc % 2][:])
                        P.inc(ie, "EV", ("EV", ph, j_e, jc, dc))

                if DEV_STAGE >= 3:
                    for c in range(NCH):
                        v_stageA(0, c)
                        if c >= 2:
                            evac(0, c - 2)
                        v_stageB(0, c)
                if DEV_STAGE >= 2:
                    for r in range(8):
                        P.wait(v, ("TRR", ph, r))
                        bank = B["pD"][1] if r % 2 == 0 else B["pCW2"][1]
                        if r < 4:
                            hi_dst = B["cwT"][0:32, r * 512:(r + 1) * 512]
                            i = v.tensor_copy(hi_dst, bank[0:32, 0:512])
                            P.selfbar(v, "VS", i)
                            i = v.tensor_sub(B["uscr"][0:32, 1, :], bank[0:32, 0:512], hi_dst)
                            P.selfbar(v, "VS", i)
                            i = v.tensor_copy(B["cwT"][32:64, r * 512:(r + 1) * 512], B["uscr"][0:32, 1, :])
                        else:
                            i = v.tensor_copy(B["cwT"][64:80, (r - 4) * 512:(r - 3) * 512],
                                              bank[0:16, 0:512])
                        P.inc(i, "RT", ("TRE", ph, r))
                    P.alias(("RTDONE", ph), ("TRE", ph, 7))
                if DEV_STAGE >= 3:
                    pend = [(0, 2), (0, 3)]
                    for i_e in range(1, NEXP):
                        for c in range(NCH):
                            v_stageA(i_e, c)
                            evac(*pend.pop(0))
                            v_stageB(i_e, c)
                            pend.append((i_e, c))
                    while pend:
                        evac(*pend.pop(0))
                binc(v)

            dispatch("ffn%d" % ph, em_sync, em_gpsimd, em_tensor, em_scalar, em_vector)

    def att_phase(A):
        ph = 100
        RL = plan_att_resources(ph)
        x1v = x1.rearrange("(k p) t -> p k t", p=128)
        x2v = x2.rearrange("(k p) t -> p k t", p=128)

        if True:
            def em_sync(sy):
                bwait(sy)
                P.inc(sy.dma_start(A["bvrep"][:], ap["bvrep"]), "ADC", None, 16)
                P.inc(sy.dma_start(A["bopp"][:], ap["bopp"]), "ADC", None, 16)
                i0 = sy.dma_start(A["mask"][:], ap["mask128"])
                P.inc(i0, "ADC", ("BV",), 16)
                for c in range(NCHA):
                    if c > 0:
                        P.wait(sy, ("LNA", ph, c - 1))
                    i1 = sy.dma_start(A["x1m"][:], x1v[:, :, c * 512:(c + 1) * 512])
                    P.inc(i1, "AD", ("X1C", ph, c), 16)
                P.wait(sy, ("LNA", ph, NCHA - 1))
                for m in range(NCHA):
                    P.wait(sy, RL.get(("x1m", ph, m)))
                    i2 = sy.dma_start(A["x1m"][:], x1v[:, :, m * 512:(m + 1) * 512])
                    P.inc(i2, "AD", ("X1M", ph, m), 16)
                    P.wait(sy, ("AX2", ph, m, 7))
                    i3 = sy.dma_start(x2v[:, :, m * 512:(m + 1) * 512], A["x1m"][:])
                    P.inc(i3, "AD", ("X2S", ph, m), 16)
                if not P.record:
                    sy.wait_ge(P.handles["AD"], P.events[("X2S", ph, NCHA - 1)][1])
                binc(sy)

            def em_gpsimd(g):
                bwait(g)
                P.inc(g.dma_start(A["wbT"][:], ap["wbT"].rearrange("(c p) r -> p c r", p=128)), "AWD", None, 16)
                i1 = g.dma_start(A["wvo"][:], ap["wvT"].rearrange("(c p) o -> p c o", p=128))
                P.inc(i1, "AWD", ("AWV",), 16)
                P.wait(g, ("VMM", ph, NCHA - 1, 3, 1))
                i2 = g.dma_start(A["wvo"][:], ap["woT"].rearrange("(c p) o -> p c o", p=128))
                P.inc(i2, "AWD", ("AWO",), 16)
                binc(g)

            def em_tensor(t):
                bwait(t)
                P.wait(t, ("AWV",))
                for c in range(NCHA):
                    P.wait(t, RL.get(("astatA", ph, c)))
                    P.wait(t, RL.get(("astatB", ph, c)))
                    for d in range(ND):
                        P.wait(t, ("aXB", ph, c, d))
                        i1 = t.matmul(A["pA"][0][0:1, :], ones_bf[:, 0:1],
                                      A["xb"][:, d % 2, :], start=(d == 0), stop=(d == ND - 1))
                        P.inc(i1, "TS", ("aSXmm", ph, c, d))
                        P.wait(t, ("aXQ", ph, c, d))
                        i2 = t.matmul(A["pA"][1][0:1, :], ones_bf[:, 0:1],
                                      A["xq"][:, 0, :], start=(d == 0), stop=(d == ND - 1))
                        P.inc(i2, "TS", ("aSQmm", ph, c, d))
                    P.alias(("aSTAT", ph, c), ("aSQmm", ph, c, ND - 1))
                    P.wait(t, ("aHILO", ph, c))
                    P.wait(t, RL.get(("arepm", ph, c)))
                    P.wait(t, RL.get(("arepi", ph, c)))
                    t.matmul(A["pA"][0][:], negones_bf[:], A["hilo"][0:1, 0, :], start=True, stop=False)
                    t.matmul(A["pA"][0][:], negones_bf[:], A["hilo"][0:1, 1, :], start=False, stop=True)
                    t.matmul(A["pA"][1][:], ones_bf[0:1, :], A["hilo"][0:1, 2, :], start=True, stop=False)
                    i3 = t.matmul(A["pA"][1][:], ones_bf[0:1, :], A["hilo"][0:1, 3, :], start=False, stop=True)
                    P.inc(i3, "TS", ("aREP", ph, c))
                    P.wait(t, ("LNA", ph, c))
                    P.wait(t, RL.get(("lat", ph, c)))
                    for d in range(ND):
                        i4 = t.matmul(A["pA"][2][0:64, :], A["wbT"][:, d, :],
                                      A["ln2"][:, d, :], start=(d == 0), stop=(d == ND - 1))
                    P.inc(i4, "TS", ("LAT", ph, c))
                    for tt in range(4):
                        for dn in range(2):
                            P.wait(t, RL.get(("vmm", ph, c, tt, dn)))
                            bank = A["pA"][3 + (tt * 2 + dn) % 2]
                            for d in range(ND):
                                i5 = t.matmul(bank[:],
                                              A["ln2"][:, d, tt * 128:(tt + 1) * 128],
                                              A["wvo"][:, d, dn * 512:(dn + 1) * 512],
                                              start=(d == 0), stop=(d == ND - 1))
                            P.inc(i5, "TS", ("VMM", ph, c, tt, dn))
                P.wait(t, ("AWO",))
                for m in range(NCHA):
                    nkc = m + 1
                    nkt = 4 * (m + 1)
                    for qt in range(4):
                        q0 = m * 512 + qt * 128
                        for kc in range(nkc):
                            P.wait(t, RL.get(("sc", ph, m, qt, kc)))
                            i6 = t.matmul(A["pA"][kc % 2][:], A["latT"][:, q0:q0 + 128],
                                          A["latT"][:, kc * 512:(kc + 1) * 512],
                                          start=True, stop=True)
                            P.inc(i6, "TS", ("SC", ph, m, qt, kc))
                        P.wait(t, ("ANORM", ph, m, qt))
                        ngr = (nkt + 3) // 4
                        for gr in range(ngr):
                            P.wait(t, RL.get(("tr", ph, m, qt, gr)))
                            bank = A["pA"][2 + gr % 2].bitcast(BF16)
                            for k in range(min(4, nkt - gr * 4)):
                                kt = gr * 4 + k
                                i7 = t.transpose(bank[:, k * 128:(k + 1) * 128],
                                                 A["attbf"][:, kt * 128:(kt + 1) * 128],
                                                 ident_bf[:])
                            P.inc(i7, "TS", ("TR2", ph, m, qt, gr))
                        P.alias(("TRLAST", ph, m, qt), ("TR2", ph, m, qt, ngr - 1))
                    for dc in range(8):
                        P.wait(t, RL.get(("av", ph, m, dc)))
                    P.wait(t, ("ATRE", ph, m))
                    for kt in range(nkt):
                        for dc in range(8):
                            i8 = t.matmul(A["pA"][dc][:],
                                          A["v"][:, kt, dc * 128:(dc + 1) * 128],
                                          A["attT"][:, kt, :],
                                          start=(kt == 0), stop=(kt == nkt - 1))
                            if kt == nkt - 1:
                                P.inc(i8, "TS", ("AAV", ph, m, dc))
                    for dc2 in range(8):
                        P.wait(t, ("AYE", ph, m, dc2))
                    for dc2 in range(8):
                        P.wait(t, RL.get(("wo", ph, m, dc2)))
                        for d in range(ND):
                            i9 = t.matmul(A["pA"][dc2 % 2][:],
                                          A["wvo"][:, d, dc2 * 128:(dc2 + 1) * 128],
                                          A["y"][:, d, :], start=(d == 0), stop=(d == ND - 1))
                        P.inc(i9, "TS", ("AWO2", ph, m, dc2))
                    P.alias(("AWO", ph, m, 7), ("AWO2", ph, m, 7))
                binc(t)

            def em_scalar(sc_):
                bwait(sc_)
                for c in range(NCHA):
                    P.wait(sc_, ("X1C", ph, c))
                    for d in range(ND):
                        P.wait(sc_, RL.get(("axq", ph, c, d)))
                        i1 = sc_.activation(A["xq"][:, 0, :], A["x1m"][:, d, :], AF.Square)
                        P.inc(i1, "SS", ("aXQ", ph, c, d))
                    P.wait(sc_, ("aVAR", ph, c))
                    i2 = sc_.activation(A["vvec"][:], A["vvec"][:], AF.Ln)
                    P.selfbar(sc_, "SS", i2)
                    i3 = sc_.mul(A["vvec"][:], A["vvec"][:], -0.5)
                    P.selfbar(sc_, "SS", i3)
                    i4 = sc_.activation(A["vvec"][:], A["vvec"][:], AF.Exp)
                    P.inc(i4, "SS", ("aINV", ph, c))
                for m in range(NCHA):
                    nk = (m + 1) * 512
                    for qt in range(4):
                        P.wait(sc_, ("AMAX", ph, m, qt))
                        P.wait(sc_, RL.get(("attbf", ph, m, qt)))
                        i5 = sc_.activation(A["attbf"][:, 0:nk], A["scrow"][:, 0:nk],
                                            AF.Exp, bias=A["stats"][:, 0:1])
                        P.inc(i5, "SS", ("AEXP", ph, m, qt))
                binc(sc_)

            def em_vector(v):
                bwait(v)
                P.wait(v, ("BV",))
                for c in range(NCHA):
                    P.wait(v, ("X1C", ph, c))
                    for d in range(ND):
                        P.wait(v, RL.get(("axb", ph, c, d)))
                        i1 = v.tensor_copy(A["xb"][:, d % 2, :], A["x1m"][:, d, :])
                        P.inc(i1, "VS", ("aXB", ph, c, d))
                    P.wait(v, ("aSTAT", ph, c))
                    v.tensor_copy(A["sx"][0:1, 0, :], A["pA"][0][0:1, :])
                    i3 = v.tensor_copy(A["sx"][0:1, 1, :], A["pA"][1][0:1, :])
                    P.inc(i3, "VS", ("aSTEV", ph, c))
                    P.wait(v, ("aSTEV", ph, c))
                    i4 = v.tensor_scalar(A["mvec"][:], A["sx"][0:1, 0, :], 1.0 / D, None, ALU.mult)
                    P.selfbar(v, "VS", i4)
                    i5 = v.tensor_mul(A["vvec"][:], A["mvec"][:], A["mvec"][:])
                    P.selfbar(v, "VS", i5)
                    i6 = v.scalar_tensor_tensor(A["vvec"][:], A["sx"][0:1, 1, :], 1.0 / D,
                                                A["vvec"][:], ALU.mult, ALU.subtract)
                    P.selfbar(v, "VS", i6)
                    i7 = v.tensor_scalar(A["vvec"][:], A["vvec"][:], 1.0, 1e-5, ALU.mult, ALU.add)
                    P.inc(i7, "VS", ("aVAR", ph, c))
                    i8 = v.tensor_copy(A["hilo"][0:1, 0, :], A["mvec"][:])
                    P.selfbar(v, "VS", i8)
                    i9 = v.tensor_sub(A["sx"][0:1, 0, :], A["mvec"][:], A["hilo"][0:1, 0, :])
                    P.selfbar(v, "VS", i9)
                    v.tensor_copy(A["hilo"][0:1, 1, :], A["sx"][0:1, 0, :])
                    P.wait(v, ("aINV", ph, c))
                    i11 = v.tensor_copy(A["hilo"][0:1, 2, :], A["vvec"][:])
                    P.selfbar(v, "VS", i11)
                    i12 = v.tensor_sub(A["sx"][0:1, 1, :], A["vvec"][:], A["hilo"][0:1, 2, :])
                    P.selfbar(v, "VS", i12)
                    i13 = v.tensor_copy(A["hilo"][0:1, 3, :], A["sx"][0:1, 1, :])
                    P.inc(i13, "VS", ("aHILO", ph, c))
                    P.wait(v, ("aREP", ph, c))
                    P.wait(v, RL.get(("ln2", ph, c)))
                    for d in range(ND):
                        i14 = v.tensor_add(A["x1m"][:, d, :], A["x1m"][:, d, :], A["pA"][0][:])
                    P.selfbar(v, "VS", i14)
                    for d in range(ND):
                        i15 = v.tensor_mul(A["ln2"][:, d, :], A["x1m"][:, d, :], A["pA"][1][:])
                    P.inc(i15, "VS", ("LNA", ph, c))
                    P.wait(v, ("LAT", ph, c))
                    i16 = v.tensor_copy(A["latT"][0:64, c * 512:(c + 1) * 512], A["pA"][2][0:64, :])
                    P.inc(i16, "VS", ("LATE", ph, c))
                    for tt in range(4):
                        for dn in range(2):
                            P.wait(v, ("VMM", ph, c, tt, dn))
                            bank = A["pA"][3 + (tt * 2 + dn) % 2]
                            i17 = v.tensor_add(A["v"][:, c * 4 + tt, dn * 512:(dn + 1) * 512],
                                               bank[:], A["bvrep"][:, dn * 512:(dn + 1) * 512])
                            P.inc(i17, "VS", ("VE", ph, c, tt, dn))
                for m in range(NCHA):
                    nkc = m + 1
                    nkt = 4 * (m + 1)
                    nk = nkc * 512
                    for qt in range(4):
                        q0 = m * 512 + qt * 128
                        P.wait(v, RL.get(("scrow", ph, m, qt)))
                        for kc in range(nkc):
                            P.wait(v, ("SC", ph, m, qt, kc))
                            i1 = v.tensor_scalar(A["scrow"][:, kc * 512:(kc + 1) * 512],
                                                 A["pA"][kc % 2][:], 0.125, None, ALU.mult)
                            P.inc(i1, "VS", ("SCE", ph, m, qt, kc))
                        P.wait(v, ("SCE", ph, m, qt, nkc - 1))
                        i2 = v.tensor_add(A["scrow"][:, q0:q0 + 128],
                                          A["scrow"][:, q0:q0 + 128], A["mask"][:])
                        if q0 + 128 < nk:
                            i2 = v.memset(A["scrow"][:, q0 + 128:nk], NEG)
                        P.selfbar(v, "VS", i2)
                        i3 = v.reduce_max(A["stats"][:, 0:1], A["scrow"][:, 0:nk],
                                          axis=AX.X, negate=True)
                        P.inc(i3, "VS", ("AMAX", ph, m, qt))
                        P.wait(v, ("AEXP", ph, m, qt))
                        i4 = v.reduce_sum(A["stats"][:, 1:2], A["attbf"][:, 0:nk], axis=AX.X)
                        P.selfbar(v, "VS", i4)
                        i5 = v.reciprocal(A["stats"][:, 2:3], A["stats"][:, 1:2])
                        P.selfbar(v, "VS", i5)
                        i6 = v.tensor_scalar(A["attbf"][:, 0:nk], A["attbf"][:, 0:nk],
                                             A["stats"][:, 2:3], None, ALU.mult)
                        P.inc(i6, "VS", ("ANORM", ph, m, qt))
                        if qt == 0:
                            P.wait(v, RL.get(("attT", ph, m)))
                        ngr = (nkt + 3) // 4
                        for gr in range(ngr):
                            P.wait(v, ("TR2", ph, m, qt, gr))
                            bank = A["pA"][2 + gr % 2].bitcast(BF16)
                            nkk = min(4, nkt - gr * 4)
                            i7 = v.tensor_copy(A["attT"][:, gr * 4:gr * 4 + nkk, qt * 128:(qt + 1) * 128],
                                               bank.rearrange("p (k f) -> p k f", f=128)[:, 0:nkk, :])
                            P.inc(i7, "VS", ("TRE2", ph, m, qt, gr))
                    P.alias(("ATRE", ph, m), ("TRE2", ph, m, 3, ((nkt + 3) // 4) - 1))
                    P.wait(v, RL.get(("ysb", ph, m)))
                    for dc in range(8):
                        P.wait(v, ("AAV", ph, m, dc))
                        i8 = v.tensor_copy(A["y"][:, dc, :], A["pA"][dc][:])
                        P.inc(i8, "VS", ("AYE", ph, m, dc))
                    P.wait(v, ("X1M", ph, m))
                    for dc2 in range(8):
                        P.wait(v, ("AWO2", ph, m, dc2))
                        P.wait(v, RL.get(("osb", ph, m, dc2)))
                        i8b = v.tensor_scalar(A["osb"][:, dc2 % 2, :], A["pA"][dc2 % 2][:],
                                              A["bopp"][:, dc2:dc2 + 1], None, ALU.add)
                        P.inc(i8b, "VS", ("ACP", ph, m, dc2))
                        P.wait(v, ("ACP", ph, m, dc2))
                        i9 = v.tensor_add(A["x1m"][:, dc2, :], A["x1m"][:, dc2, :],
                                          A["osb"][:, dc2 % 2, :])
                        P.inc(i9, "VS", ("AX2", ph, m, dc2))
                binc(v)

            dispatch("att", em_sync, em_gpsimd, em_tensor, em_scalar, em_vector)

    _arena_ctr = [0]

    def ffn_arena(fes):
        _arena_ctr[0] += 1
        tag = "ffn%d_" % _arena_ctr[0]

        def fsb(name, shape, dt):
            return fes.enter_context(nc.sbuf_tensor(tag + name, list(shape), dt))
        return dict(
            x_sb=fsb("x", (128, ND, SEG), F32),
            ln_sb=fsb("ln", (128, ND, SEG), BF16),
            wgb=[fsb("wg%d" % p, (128, ND, 512), BF16) for p in range(2)],
            wub=[fsb("wu%d" % p, (128, ND, 512), BF16) for p in range(2)],
            wdb=[fsb("wd%d" % p, (128, 4, 1024), BF16) for p in range(2)],
            h1=fsb("h1", (128, 4, 512), BF16),
            ssil=fsb("ssil", (128, 4, 512), BF16),
            h=fsb("h", (128, 2, 4, 512), BF16),
            xb=fsb("xb", (128, 2, 512), BF16),
            xq=fsb("xq", (128, 1, 512), BF16),
            cwrep=fsb("cwrep", (128, 1, 512), F32),
            uscr=fsb("uscr", (128, 4, 512), F32),
            sc=fsb("scw", (128, 512), F32),
            cwT=fsb("cwT", (128, SEG), BF16),
            sx=fsb("sx", (1, 2, 512), F32),
            mvec=fsb("mvec", (1, 512), F32),
            vvec=fsb("vvec", (1, 512), F32),
            hilo=fsb("hilo", (1, 4, 512), BF16),
            st=fsb("st", (128, 2, 16), F32),
            gt=fsb("gt", (128, 8, 64), F32),
            idx0=fsb("idx0", (128, 16), F32),
            ohsel32=fsb("oh32", (128, 32 * 128), BF16),
            iota32w=fsb("i32", (128, 512), BF16),
            iota8w=fsb("i8", (128, 512), BF16),
            iota4w=fsb("i4", (128, 64), BF16),
            gbw=fsb("gbw", (128, 512), BF16),
            pG=[banks[0], banks[1]],
            pU=[banks[2], banks[3]],
            pD=[banks[4], banks[5]],
            pCW2=[banks[6], banks[7]],
        )

    def emit_all():
        bar["target"] = 0
        nphase = 0
        first = True
        if "f1" in phases:
            with ExitStack() as fes:
                FB = ffn_arena(fes)
                for s_ in range(NSEG):
                    ffn_seg(0 + s_, s_, ap["xT"], x1, FB, first)
                    first = False
                    nphase += 1
                    bar["target"] = 5 * nphase
        if "att" in phases:
            with ExitStack() as aes:
                _arena_ctr[0] += 1
                _atag = "att%d_" % _arena_ctr[0]

                def asb(name, shape, dt):
                    return aes.enter_context(nc.sbuf_tensor(_atag + name, list(shape), dt))
                AB = dict(
                    v=asb("v", (128, 32, 1024), BF16),
                    attT=asb("attT", (128, 32, 512), BF16),
                    latT=asb("latT", (64, T), BF16),
                    ln2=asb("ln2", (128, ND, 512), BF16),
                    x1m=asb("x1m", (128, ND, 512), F32),
                    wvo=asb("wvo", (128, ND, 1024), BF16),
                    wbT=asb("wbT", (128, ND, 64), BF16),
                    bvrep=asb("bvrep", (128, D), F32),
                    bopp=asb("bopp", (128, 8), F32),
                    mask=asb("mask", (128, 128), F32),
                    scrow=asb("scrow", (128, T), F32),
                    attbf=asb("attbf", (128, T), BF16),
                    y=asb("y", (128, 8, 512), BF16),
                    osb=asb("o", (128, 2, 512), F32),
                    stats=asb("stats", (128, 4), F32),
                    xb=asb("xb", (128, 2, 512), BF16),
                    xq=asb("xq", (128, 1, 512), BF16),
                    sx=asb("sx", (1, 2, 512), F32),
                    mvec=asb("mvec", (1, 512), F32),
                    vvec=asb("vvec", (1, 512), F32),
                    hilo=asb("hilo", (1, 4, 512), BF16),
                    pA=banks,
                )
                att_phase(AB)
            nphase += 1
            bar["target"] = 5 * nphase
        if "f2" in phases:
            with ExitStack() as fes:
                FB = ffn_arena(fes)
                for s_ in range(NSEG):
                    ffn_seg(200 + s_, s_, x2, outT, FB, first)
                    first = False
                    nphase += 1
                    bar["target"] = 5 * nphase

    P.start_pass(True)
    emit_all()
    P.start_pass(False)
    emit_all()
    es.close()
    return nc


def _consts():
    c = {}
    c["ones128"] = np.ones((128, 128), dtype=np.float32)
    c["negones"] = -np.ones((1, 128), dtype=np.float32)
    c["ident"] = np.eye(128, dtype=np.float32)
    oh32 = np.zeros((128, 32 * 128), dtype=np.float32)
    for e in range(32):
        oh32[e, e * 128:(e + 1) * 128] = 1.0
        oh32[32 + e, e * 128:(e + 1) * 128] = 1.0
    for l in range(16):
        oh32[64 + l, l * 128:(l + 1) * 128] = 1.0
    c["ohsel32"] = oh32
    i32 = np.tile(np.arange(32, dtype=np.float32), 16)
    c["iota32w"] = np.ascontiguousarray(np.tile(i32, (128, 1)))
    i8 = np.tile(np.arange(8, dtype=np.float32), 64)
    c["iota8w"] = np.ascontiguousarray(np.tile(i8, (128, 1)))
    i4 = np.tile(np.arange(4, dtype=np.float32), 16)
    c["iota4w"] = np.ascontiguousarray(np.tile(i4, (128, 1)))
    m = np.zeros((128, 128), dtype=np.float32)
    for p_ in range(128):
        m[p_, p_ + 1:] = NEG
    c["mask128"] = m
    return c


def make_in_maps(inputs):
    x = np.asarray(inputs["L0_attn_hidden_states"], dtype=np.float32)
    gb = np.asarray(inputs["L0_ffn_gate_bias"], dtype=np.float32)
    consts = _consts()
    shared = dict(
        wbT=np.ascontiguousarray(np.asarray(inputs["L0_attn_W_basis"], np.float32).T),
        wvT=np.ascontiguousarray(np.asarray(inputs["L0_attn_W_v"], np.float32).T),
        woT=np.ascontiguousarray(np.asarray(inputs["L0_attn_W_o"], np.float32).T),
        bvrep=np.ascontiguousarray(np.tile(np.asarray(inputs["L0_attn_b_v"], np.float32)[None, :], (128, 1))),
        bopp=np.ascontiguousarray(np.asarray(inputs["L0_attn_b_o"], np.float32).reshape(8, 128).T),
        gwT=np.ascontiguousarray(np.asarray(inputs["L0_ffn_gate_weight"], np.float32).T),
        gbw=np.ascontiguousarray(np.tile(np.tile(gb, 16)[None, :], (128, 1))),
        wg=np.ascontiguousarray(np.asarray(inputs["L0_ffn_w_gate"], np.float32)),
        wu=np.ascontiguousarray(np.asarray(inputs["L0_ffn_w_up"], np.float32)),
        wd=np.ascontiguousarray(np.asarray(inputs["L0_ffn_w_down"], np.float32)),
        swg=np.ascontiguousarray(np.asarray(inputs["L0_ffn_shared_w_gate"], np.float32)),
        swu=np.ascontiguousarray(np.asarray(inputs["L0_ffn_shared_w_up"], np.float32)),
        swd=np.ascontiguousarray(np.asarray(inputs["L0_ffn_shared_w_down"], np.float32)),
        awg=np.ascontiguousarray(np.asarray(inputs["L0_ffn_anchor_w_gate"], np.float32)),
        awu=np.ascontiguousarray(np.asarray(inputs["L0_ffn_anchor_w_up"], np.float32)),
        awd=np.ascontiguousarray(np.asarray(inputs["L0_ffn_anchor_w_down"], np.float32)),
        **consts,
    )
    in_maps = []
    for b in range(8):
        m = dict(shared)
        m["xT"] = np.ascontiguousarray(x[b].T)
        in_maps.append(m)
    return in_maps


def kernel(**inputs):
    in_maps = make_in_maps(inputs)
    nc = build_program()
    res = run_bass_kernel_spmd(nc, in_maps, core_ids=list(range(8)))
    out = np.stack([np.asarray(r["outT"]).T for r in res.results], axis=0)
    return out.astype(np.float32)


# version 03
# speedup vs baseline: 1.1473x; 1.1473x over previous
import numpy as np
import ml_dtypes
import concourse.bass as bass
import concourse.mybir as mybir
from concourse.bass_utils import run_bass_kernel_spmd
from contextlib import ExitStack

F32 = mybir.dt.float32
BF16 = mybir.dt.bfloat16
AF = mybir.ActivationFunctionType
ALU = mybir.AluOpType
AX = mybir.AxisListType

T = 4096
D = 1024
SEG = 2048
NSEG = T // SEG
NCH = SEG // 512
ND = D // 128
NE = 32
NL = 16
B23 = 8388608.0
NEG = -1.0e30

ENTRIES = [("s", 0, 4)] + [("r", e, 4) for e in range(NE)] + [("a", l, 2) for l in range(NL)]
DEV_STAGE = 3
NEXP = len(ENTRIES)
NCHA = T // 512


class _DummyInstr:
    def then_inc(self, *a, **k):
        return self

_DI = _DummyInstr()

class _DummyEng:
    def __getattr__(self, name):
        return lambda *a, **k: _DI


class Plan:
    def __init__(self):
        self.record = True
        self.events = {}
        self.counts = {}
        self.handles = {}

    def start_pass(self, record):
        self.record = record
        self.counts = {}

    def inc(self, instr, semname, key=None, n=1):
        c = self.counts.get(semname, 0) + n
        self.counts[semname] = c
        if key is not None:
            if self.record:
                assert key not in self.events, f"dup event {key}"
                self.events[key] = (semname, c)
            else:
                assert self.events.get(key) == (semname, c), \
                    f"plan mismatch {key}: {self.events.get(key)} vs {(semname, c)}"
        if (not self.record) and (instr is not None) and not isinstance(instr, _DummyInstr):
            instr.then_inc(self.handles[semname], n)
        return instr

    def alias(self, key2, key1):
        if self.record:
            assert key1 in self.events, f"alias target missing {key1}"
            self.events[key2] = self.events[key1]

    def wait(self, eng, key):
        if key is None:
            return
        if self.record:
            return
        assert key in self.events, f"missing event {key}"
        semname, v = self.events[key]
        eng.wait_ge(self.handles[semname], v)

    def selfbar(self, eng, semname, last_instr):
        c = self.counts.get(semname, 0) + 1
        self.counts[semname] = c
        if not self.record:
            if last_instr is not None and not isinstance(last_instr, _DummyInstr):
                last_instr.then_inc(self.handles[semname], 1)
                eng.wait_ge(self.handles[semname], c)


class Ledger:

    def __init__(self):
        self.rel = {}
        self.last = {}

    def use(self, res, use_key, release_key):
        self.rel[use_key] = self.last.get(res)
        self.last[res] = release_key

    def get(self, use_key):
        return self.rel[use_key]


def plan_ffn_resources(ph):
    L = Ledger()
    for c in range(NCH):
        for d in range(ND):
            L.use("xb%d" % (d % 2), ("xb", ph, c, d), ("SXmm", ph, c, d))
            L.use("xq", ("xq", ph, c, d), ("SQmm", ph, c, d))
        L.use("pG0", ("statA", ph, c), ("STEV", ph, c))
        L.use("pG1", ("statB", ph, c), ("STEV", ph, c))
        L.use("pU0", ("repm", ph, c), ("LNB", ph, c))
        L.use("pU1", ("repi", ph, c), ("LNB", ph, c))
        for q in range(4):
            L.use("pD0", ("log", ph, c, q), ("SIG", ph, c, q))
    def use_gu(i, c, nic):
        for ic in range(nic):
            L.use("pG%d" % (ic % 2), ("g", ph, i, c, ic), ("SIL", ph, i, c, ic))
            L.use("pU%d" % (ic % 2), ("u", ph, i, c, ic), ("VA", ph, i, c, ic))
        L.use("ssil", ("ssil", ph, i, c), ("VA", ph, i, c, nic - 1))
        L.use("h%d" % (c % 2), ("hbuf", ph, i, c), ("DD", ph, i, c, 7))

    def use_down(i, c):
        for dc in range(8):
            L.use("pD%d" % (dc % 2), ("d", ph, i, c, dc), ("EV", ph, i, c, dc))

    nic0 = ENTRIES[0][2]
    for c in range(NCH):
        use_gu(0, c, nic0)
    use_down(0, 0)
    use_down(0, 1)
    for r in range(8):
        L.use("pD1" if r % 2 == 0 else "pCW1", ("trr", ph, r), ("TRE", ph, r))
    pend = [(0, 2), (0, 3)]
    for i in range(1, NEXP):
        kind, _, nic = ENTRIES[i]
        for c in range(NCH):
            if kind != "s":
                L.use("pCW%d" % ((i * NCH + c) % 2), ("cwrep", ph, i, c), ("CWE", ph, i, c))
                L.use("cwsb", ("cwsb", ph, i, c), ("VB", ph, i, c))
            use_gu(i, c, nic)
            use_down(*pend.pop(0))
            pend.append((i, c))
    while pend:
        use_down(*pend.pop(0))
    for i in range(NEXP):
        L.use("wbuf%d" % (i % 2), ("wbuf", ph, i), ("WF", ph, i))
    return L


def plan_att_resources(ph):
    L = Ledger()
    for c in range(NCHA):
        for d in range(ND):
            L.use("axb%d" % (d % 2), ("axb", ph, c, d), ("aSXmm", ph, c, d))
            L.use("axq", ("axq", ph, c, d), ("aSQmm", ph, c, d))
        L.use("pA0", ("astatA", ph, c), ("aSTEV", ph, c))
        L.use("pA1", ("astatB", ph, c), ("aSTEV", ph, c))
        L.use("pA0", ("arepm", ph, c), ("LNA", ph, c))
        L.use("pA1", ("arepi", ph, c), ("LNA", ph, c))
        L.use("ln2", ("ln2", ph, c), ("VMM", ph, c, 3, 1))
        L.use("pA2", ("lat", ph, c), ("LATE", ph, c))
        for tt in range(4):
            for dn in range(2):
                L.use("pA%d" % (3 + (tt * 2 + dn) % 2), ("vmm", ph, c, tt, dn),
                      ("VE", ph, c, tt, dn))
    for m in range(NCHA):
        nkc = m + 1
        nkt = 4 * (m + 1)
        for qt in range(4):
            for kc in range(nkc):
                L.use("pA%d" % (kc % 2), ("sc", ph, m, qt, kc), ("SCE", ph, m, qt, kc))
            ngr = (nkt + 3) // 4
            for gr in range(ngr):
                L.use("pA%d" % (2 + gr % 2), ("tr", ph, m, qt, gr), ("TRE2", ph, m, qt, gr))
            L.use("scrow", ("scrow", ph, m, qt), ("AEXP", ph, m, qt))
            L.use("attbf", ("attbf", ph, m, qt), ("TRLAST", ph, m, qt))
        L.use("attT", ("attT", ph, m), ("AAV", ph, m, 7))
        for dc in range(8):
            L.use("pA%d" % dc, ("av", ph, m, dc), ("AYE", ph, m, dc))
        L.use("ysb", ("ysb", ph, m), ("AWO", ph, m, 7))
        for dc2 in range(8):
            L.use("pA%d" % (dc2 % 2), ("wo", ph, m, dc2), ("ACP", ph, m, dc2))
            L.use("osb%d" % (dc2 % 2), ("osb", ph, m, dc2), ("AX2", ph, m, dc2))
        L.use("x1m", ("x1m", ph, m), ("X2S", ph, m))
    return L


def pe_ffn_order():
    seq = []
    pend = []
    for i in range(NEXP):
        for c in range(NCH):
            seq.append(("gu", i, c))
            pend.append((i, c))
            if len(pend) > 1:
                seq.append(("down",) + pend.pop(0))
    while pend:
        seq.append(("down",) + pend.pop(0))
    return seq


def build_program(phases=("f1", "att", "f2"), dev_expose=()):
    nc = bass.Bass("TRN2", target_bir_lowering=False)
    P = Plan()

    ap = {}

    def din(name, shape):
        ap[name] = nc.dram_tensor(name, list(shape), F32, kind="ExternalInput").ap()

    din("xT", (D, T))
    din("wbT", (128, 512)); din("wvT", (128, 8 * D)); din("woT", (128, 8 * D))
    din("bvrep", (128, D)); din("bopp", (128, 8))
    din("gwT", (D, 32)); din("gbw", (128, 512))
    din("wall_r", (NE + 1, 128, 12288))
    din("wall_a", (NL, 128, 6144))
    din("ones128", (128, 128)); din("negones", (1, 128)); din("ident", (128, 128))
    din("ohsel32", (128, 32 * 128))
    din("iota32w", (128, 512)); din("iota8w", (128, 512)); din("iota4w", (128, 64))
    din("mask128", (128, 128))

    x1 = nc.dram_tensor("x1", [D, T], F32,
                        kind="ExternalOutput" if "x1" in dev_expose else "Internal").ap()
    x2 = nc.dram_tensor("x2", [D, T], F32,
                        kind="ExternalOutput" if "x2" in dev_expose else "Internal").ap()
    outT = nc.dram_tensor("outT", [D, T], F32, kind="ExternalOutput").ap()

    es = ExitStack()

    def sb(name, shape, dt):
        return es.enter_context(nc.sbuf_tensor(name, list(shape), dt))

    ones_bf = sb("ones_bf", (128, 128), BF16)
    negones_bf = sb("negones_bf", (1, 128), BF16)
    ident_f = sb("ident_f", (128, 128), F32)
    ident_bf = sb("ident_bf", (128, 128), BF16)
    gwT_sb = sb("gwT_sb", (128, ND, 32), BF16)
    banks = [es.enter_context(nc.psum_tensor("bank%d" % p, [128, 512], F32))
             for p in range(8)]

    semnames = ["WD0", "WD1", "CWD", "AWD", "ADC", "XD", "XS", "CWR", "CWE",
                "GG", "UU", "SIL", "VA", "VB", "DD", "EV", "VS", "SS", "TS",
                "PB", "RT", "AD"]
    for s_ in semnames:
        P.handles[s_] = es.enter_context(nc.semaphore(s_))

    bar = {"target": 0}

    def bwait(eng):
        if bar["target"] > 0 and not P.record:
            eng.wait_ge(P.handles["PB"], bar["target"])

    def binc(eng):
        P.counts["PB"] = P.counts.get("PB", 0) + 1
        if not P.record:
            eng.sem_inc(P.handles["PB"], 1)

    def dispatch(name, em_sync, em_gpsimd, em_tensor, em_scalar, em_vector):
        if P.record:
            em_sync(_DummyEng())
            em_gpsimd(_DummyEng())
            em_tensor(_DummyEng())
            em_scalar(_DummyEng())
            em_vector(_DummyEng())
        else:
            with nc.named_scope(name):
                with nc.Block(name) as block:
                    block.sync(em_sync)
                    block.gpsimd(em_gpsimd)
                    block.tensor(em_tensor)
                    block.scalar(em_scalar)
                    block.vector(em_vector)

    def ffn_seg(ph, s_, src_dram, dst_dram, B, first_seg_of_program):
        RL = plan_ffn_resources(ph)
        B["cwtm"] = B["uscr"][:, 0, :]
        B["acwtm"] = B["uscr"][:, 2, 0:256]

        tok0 = s_ * SEG
        x_src = src_dram.rearrange("(c p) t -> p c t", p=128)[:, :, tok0:tok0 + SEG]
        x_dst = dst_dram.rearrange("(c p) t -> p c t", p=128)[:, :, tok0:tok0 + SEG]

        if True:
            def em_sync(sy):
                bwait(sy)
                i1 = sy.dma_start(B["x_sb"][:], x_src)
                P.inc(i1, "XD", ("XD", ph), 16)
                if DEV_STAGE >= 3:
                    P.wait(sy, ("EV", ph, NEXP - 1, NCH - 1, 7))
                elif DEV_STAGE == 2:
                    P.wait(sy, ("RTDONE", ph))
                elif DEV_STAGE == 1:
                    P.wait(sy, ("LNB", ph, NCH - 1))
                else:
                    if not P.record:
                        sy.wait_ge(P.handles["XD"], P.events[("XD", ph)][1])
                i2 = sy.dma_start(x_dst, B["x_sb"][:])
                P.inc(i2, "XS", ("XS", ph), 16)
                if not P.record:
                    sy.wait_ge(P.handles["XS"], P.events[("XS", ph)][1])
                binc(sy)

            def em_gpsimd(g):
                bwait(g)
                if first_seg_of_program:
                    P.inc(g.dma_start(ones_bf[:], ap["ones128"]), "CWD", None, 16)
                    P.inc(g.dma_start(negones_bf[:], ap["negones"]), "CWD", None, 16)
                    P.inc(g.dma_start(ident_f[:], ap["ident"]), "CWD", None, 16)
                    P.inc(g.dma_start(ident_bf[:], ap["ident"]), "CWD", None, 16)
                    P.inc(g.dma_start(gwT_sb[:], ap["gwT"].rearrange("(c p) e -> p c e", p=128)), "CWD", None, 16)
                if s_ == 0:
                    P.inc(g.dma_start(B["ohsel32"][:], ap["ohsel32"]), "CWD", None, 16)
                    P.inc(g.dma_start(B["iota32w"][:], ap["iota32w"]), "CWD", None, 16)
                    P.inc(g.dma_start(B["iota8w"][:], ap["iota8w"]), "CWD", None, 16)
                    P.inc(g.dma_start(B["iota4w"][:], ap["iota4w"]), "CWD", None, 16)
                    i0 = g.dma_start(B["gbw"][:], ap["gbw"])
                    P.inc(i0, "CWD", ("CONSTS", ph), 16)
                for i, (kind, e, nic) in enumerate(ENTRIES):
                    if DEV_STAGE < 3:
                        break
                    p = i % 2
                    P.wait(g, RL.get(("wbuf", ph, i)))
                    if kind == "a":
                        srcw = ap["wall_a"][e]
                        dw = B["wb"][p][:, 0:6144]
                    else:
                        srcw = ap["wall_r"][NE if kind == "s" else e]
                        dw = B["wb"][p][:, 0:12288]
                    wsem = "WD%d" % (i % 2)
                    P.inc(g.dma_start(dw, srcw), wsem, ("WDall", ph, i), 16)
                binc(g)

            def em_tensor(t):
                bwait(t)
                if s_ == 0:
                    P.wait(t, ("CONSTS", ph))
                for c in range(NCH):
                    if DEV_STAGE < 1:
                        break
                    P.wait(t, RL.get(("statA", ph, c)))
                    P.wait(t, RL.get(("statB", ph, c)))
                    for d in range(ND):
                        P.wait(t, ("XB", ph, c, d))
                        i1 = t.matmul(B["pG"][0][0:1, :], ones_bf[:, 0:1],
                                      B["xb"][:, d % 2, :], start=(d == 0), stop=(d == ND - 1))
                        P.inc(i1, "TS", ("SXmm", ph, c, d))
                        P.wait(t, ("XQ", ph, c, d))
                        i2 = t.matmul(B["pG"][1][0:1, :], ones_bf[:, 0:1],
                                      B["xq"][:, 0, :], start=(d == 0), stop=(d == ND - 1))
                        P.inc(i2, "TS", ("SQmm", ph, c, d))
                    P.alias(("STAT", ph, c), ("SQmm", ph, c, ND - 1))
                    P.wait(t, ("HILO", ph, c))
                    P.wait(t, RL.get(("repm", ph, c)))
                    P.wait(t, RL.get(("repi", ph, c)))
                    t.matmul(B["pU"][0][:], negones_bf[:], B["hilo"][0:1, 0, :], start=True, stop=False)
                    t.matmul(B["pU"][0][:], negones_bf[:], B["hilo"][0:1, 1, :], start=False, stop=True)
                    t.matmul(B["pU"][1][:], ones_bf[0:1, :], B["hilo"][0:1, 2, :], start=True, stop=False)
                    i3 = t.matmul(B["pU"][1][:], ones_bf[0:1, :], B["hilo"][0:1, 3, :], start=False, stop=True)
                    P.inc(i3, "TS", ("REP", ph, c))
                    P.wait(t, ("LNB", ph, c))
                    for q in range(4):
                        if DEV_STAGE < 2:
                            break
                        P.wait(t, RL.get(("log", ph, c, q)))
                        for d in range(ND):
                            i4 = t.matmul(B["pD"][0][:, q * 32:(q + 1) * 32],
                                          B["ln_sb"][:, d, c * 512 + q * 128: c * 512 + (q + 1) * 128],
                                          gwT_sb[:, d, :], start=(d == 0), stop=(d == ND - 1))
                        P.inc(i4, "TS", ("LOG", ph, c, q))
                def t_gu(i, c, prev):
                    kind, e, nic = ENTRIES[i]
                    p = i % 2
                    if prev is not None:
                        pi, pc = prev
                        pn = ENTRIES[pi][2]
                        P.wait(t, ("SIL", ph, pi, pc, pn - 1))
                        P.wait(t, ("VA", ph, pi, pc, pn - 1))
                    if c == 0:
                        P.wait(t, ("WDall", ph, i))
                    isz = 256 if kind == "a" else 512
                    uoff = 2048 if kind == "a" else 4096
                    for ic in range(nic):
                        if prev is None:
                            P.wait(t, RL.get(("g", ph, i, c, ic)))
                            P.wait(t, RL.get(("u", ph, i, c, ic)))
                        elif ic >= 2:
                            P.wait(t, ("SIL", ph, i, c, ic - 2))
                            P.wait(t, ("VA", ph, i, c, ic - 2))
                        for d in range(ND):
                            o_ = d * isz + ic * 128
                            i7 = t.matmul(B["pG"][ic % 2][:],
                                          B["wb"][p][:, o_:o_ + 128],
                                          B["ln_sb"][:, d, c * 512:(c + 1) * 512],
                                          start=(d == 0), stop=(d == ND - 1))
                        P.inc(i7, "GG", ("GG", ph, i, c, ic))
                        for d in range(ND):
                            o_ = uoff + d * isz + ic * 128
                            i8 = t.matmul(B["pU"][ic % 2][:],
                                          B["wb"][p][:, o_:o_ + 128],
                                          B["ln_sb"][:, d, c * 512:(c + 1) * 512],
                                          start=(d == 0), stop=(d == ND - 1))
                        P.inc(i8, "UU", ("UU", ph, i, c, ic))

                def t_cwrep(i, c):
                    kind, e, nic = ENTRIES[i]
                    if kind == "s":
                        return
                    bank = B["pCW2"][(i * NCH + c) % 2]
                    P.wait(t, RL.get(("cwrep", ph, i, c)))
                    if kind == "r":
                        i6 = t.matmul(bank[:], B["ohsel32"][0:64, e * 128:(e + 1) * 128],
                                      B["cwT"][0:64, c * 512:(c + 1) * 512], start=True, stop=True)
                    else:
                        i6 = t.matmul(bank[:], B["ohsel32"][64:80, e * 128:(e + 1) * 128],
                                      B["cwT"][64:80, c * 512:(c + 1) * 512], start=True, stop=True)
                    P.inc(i6, "CWR", ("CWR", ph, i, c))

                def t_down(i, c, prevd):
                    kind, e, nic = ENTRIES[i]
                    p = i % 2
                    P.wait(t, ("VB", ph, i, c))
                    if prevd is not None:
                        P.wait(t, ("EV", ph, prevd[0], prevd[1], 7))
                    doff = 4096 if kind == "a" else 8192
                    for dc in range(8):
                        if prevd is None:
                            P.wait(t, RL.get(("d", ph, i, c, dc)))
                        elif dc >= 2:
                            P.wait(t, ("EV", ph, i, c, dc - 2))
                        for ic in range(nic):
                            o_ = doff + ic * 1024 + dc * 128
                            i9 = t.matmul(B["pD"][dc % 2][:],
                                          B["wb"][p][:, o_:o_ + 128],
                                          B["h"][:, c % 2, ic, :],
                                          start=(ic == 0), stop=(ic == nic - 1))
                        P.inc(i9, "DD", ("DD", ph, i, c, dc))
                    if c == NCH - 1:
                        P.alias(("WF", ph, i), ("DD", ph, i, c, 7))

                if DEV_STAGE >= 3:
                    prev_gu = None
                    prevd = None
                    for c in range(NCH):
                        t_gu(0, c, prev_gu)
                        prev_gu = (0, c)
                        if c >= 2:
                            t_down(0, c - 2, prevd)
                            prevd = (0, c - 2)
                if DEV_STAGE >= 2:
                    P.wait(t, ("RT_cw", ph))
                    for r in range(8):
                        P.wait(t, RL.get(("trr", ph, r)))
                        bank = B["pD"][1] if r % 2 == 0 else B["pCW2"][1]
                        for k in range(4):
                            sl = r * 4 + k
                            if r < 4:
                                i5 = t.transpose(bank[0:32, k * 128:(k + 1) * 128],
                                                 B["cwtm"][:, sl * 32:(sl + 1) * 32], ident_f[:])
                            else:
                                i5 = t.transpose(bank[0:16, k * 128:(k + 1) * 128],
                                                 B["acwtm"][:, (sl - 16) * 16:(sl - 15) * 16], ident_f[:])
                        P.inc(i5, "TS", ("TRR", ph, r))
                if DEV_STAGE >= 3:
                    pend = [(0, 2), (0, 3)]
                    for i in range(1, NEXP):
                        for c in range(NCH):
                            t_cwrep(i, c)
                            t_gu(i, c, prev_gu)
                            prev_gu = (i, c)
                            jj = pend.pop(0)
                            t_down(*jj, prevd)
                            prevd = jj
                            pend.append((i, c))
                    while pend:
                        jj = pend.pop(0)
                        t_down(*jj, prevd)
                        prevd = jj
                binc(t)

            def em_scalar(sc_):
                bwait(sc_)
                P.wait(sc_, ("XD", ph))
                for c in range(NCH):
                    if DEV_STAGE < 1:
                        break
                    for d in range(ND):
                        P.wait(sc_, RL.get(("xq", ph, c, d)))
                        i1 = sc_.activation(B["xq"][:, 0, :],
                                            B["x_sb"][:, d, c * 512:(c + 1) * 512], AF.Square)
                        P.inc(i1, "SS", ("XQ", ph, c, d))
                    P.wait(sc_, ("VAR", ph, c))
                    i2 = sc_.activation(B["vvec"][:], B["vvec"][:], AF.Ln)
                    P.selfbar(sc_, "SS", i2)
                    i3 = sc_.mul(B["vvec"][:], B["vvec"][:], -0.5)
                    P.selfbar(sc_, "SS", i3)
                    i4 = sc_.activation(B["vvec"][:], B["vvec"][:], AF.Exp)
                    P.inc(i4, "SS", ("INV", ph, c))
                    for q in range(4):
                        if DEV_STAGE < 2:
                            break
                        P.wait(sc_, ("LOG", ph, c, q))
                        i5 = sc_.activation(B["sc"][:, (c * 4 + q) * 32:(c * 4 + q + 1) * 32],
                                            B["pD"][0][:, q * 32:(q + 1) * 32], AF.Sigmoid)
                        P.inc(i5, "SS", ("SIG", ph, c, q))
                def sc_cwevac(i, c):
                    kind, e, nic = ENTRIES[i]
                    if kind == "s":
                        return
                    P.wait(sc_, ("CWR", ph, i, c))
                    P.wait(sc_, RL.get(("cwsb", ph, i, c)))
                    i6 = sc_.copy(B["cwrep"][:, 0, :], B["pCW2"][(i * NCH + c) % 2][:])
                    P.inc(i6, "CWE", ("CWE", ph, i, c))

                def sc_silu(i, c):
                    kind, e, nic = ENTRIES[i]
                    P.wait(sc_, RL.get(("ssil", ph, i, c)))
                    for ic in range(nic):
                        P.wait(sc_, ("GG", ph, i, c, ic))
                        i7 = sc_.activation(B["ssil"][:, ic, :], B["pG"][ic % 2][:], AF.Silu)
                        P.inc(i7, "SIL", ("SIL", ph, i, c, ic))

                if DEV_STAGE >= 3:
                    for c in range(NCH):
                        sc_silu(0, c)
                    for c in range(NCH):
                        sc_cwevac(0, c)
                    for i in range(1, NEXP):
                        for c in range(NCH):
                            sc_cwevac(i, c)
                            sc_silu(i, c)
                binc(sc_)

            def em_vector(v):
                bwait(v)
                P.wait(v, ("XD", ph))
                for c in range(NCH):
                    if DEV_STAGE < 1:
                        break
                    for d in range(ND):
                        P.wait(v, RL.get(("xb", ph, c, d)))
                        i1 = v.tensor_copy(B["xb"][:, d % 2, :],
                                           B["x_sb"][:, d, c * 512:(c + 1) * 512])
                        P.inc(i1, "VS", ("XB", ph, c, d))
                    P.wait(v, ("STAT", ph, c))
                    v.tensor_copy(B["sx"][0:1, 0, :], B["pG"][0][0:1, :])
                    i3 = v.tensor_copy(B["sx"][0:1, 1, :], B["pG"][1][0:1, :])
                    P.inc(i3, "VS", ("STEV", ph, c))
                    P.wait(v, ("STEV", ph, c))
                    i4 = v.tensor_scalar(B["mvec"][:], B["sx"][0:1, 0, :], 1.0 / D, None, ALU.mult)
                    P.selfbar(v, "VS", i4)
                    i5 = v.tensor_mul(B["vvec"][:], B["mvec"][:], B["mvec"][:])
                    P.selfbar(v, "VS", i5)
                    i6 = v.scalar_tensor_tensor(B["vvec"][:], B["sx"][0:1, 1, :], 1.0 / D,
                                                B["vvec"][:], ALU.mult, ALU.subtract)
                    P.selfbar(v, "VS", i6)
                    i7 = v.tensor_scalar(B["vvec"][:], B["vvec"][:], 1.0, 1e-5, ALU.mult, ALU.add)
                    P.inc(i7, "VS", ("VAR", ph, c))
                    i8 = v.tensor_copy(B["hilo"][0:1, 0, :], B["mvec"][:])
                    P.selfbar(v, "VS", i8)
                    i9 = v.tensor_sub(B["sx"][0:1, 0, :], B["mvec"][:], B["hilo"][0:1, 0, :])
                    P.selfbar(v, "VS", i9)
                    v.tensor_copy(B["hilo"][0:1, 1, :], B["sx"][0:1, 0, :])
                    P.wait(v, ("INV", ph, c))
                    i11 = v.tensor_copy(B["hilo"][0:1, 2, :], B["vvec"][:])
                    P.selfbar(v, "VS", i11)
                    i12 = v.tensor_sub(B["sx"][0:1, 1, :], B["vvec"][:], B["hilo"][0:1, 2, :])
                    P.selfbar(v, "VS", i12)
                    i13 = v.tensor_copy(B["hilo"][0:1, 3, :], B["sx"][0:1, 1, :])
                    P.inc(i13, "VS", ("HILO", ph, c))
                    P.wait(v, ("REP", ph, c))
                    for dh in range(2):
                        for d4 in range(4):
                            d = dh * 4 + d4
                            i14 = v.tensor_add(B["uscr"][:, d4, :],
                                               B["x_sb"][:, d, c * 512:(c + 1) * 512], B["pU"][0][:])
                        P.selfbar(v, "VS", i14)
                        for d4 in range(4):
                            d = dh * 4 + d4
                            i15 = v.tensor_mul(B["ln_sb"][:, d, c * 512:(c + 1) * 512],
                                               B["uscr"][:, d4, :], B["pU"][1][:])
                        if dh == 0:
                            P.selfbar(v, "VS", i15)
                    P.inc(i15, "VS", ("LNB", ph, c))
                if DEV_STAGE >= 2:
                  for q in range(4):
                    P.wait(v, ("SIG", ph, NCH - 1, q))
                if DEV_STAGE >= 2:
                    scv = B["sc"][:]
                    scb = B["uscr"][:, 0, :]
                    msc = B["uscr"][:, 1, :]
                    tmp1 = B["uscr"][:, 2, :]
                    tmp2 = B["uscr"][:, 3, :]
                    gt = B["gt"]
                    st = B["st"]
                    selm = gt[:].rearrange("p a c -> p (a c)")

                    def v3(apx, k):
                        return apx.rearrange("p (a c) -> p a c", c=k)

                    def bc(apx, g_, k):
                        return apx.to_broadcast([128, g_, k])

                    def VB_(i):
                        P.selfbar(v, "VS", i)

                    i = v.tensor_add(scb, scv, B["gbw"][:]); VB_(i)
                    i = v.reduce_max(gt[:, 0, :], v3(scb, 8), axis=AX.X); VB_(i)
                    i = v.tensor_tensor(v3(tmp1, 8), v3(scb, 8), bc(gt[:, 0, :], 64, 8), ALU.is_equal); VB_(i)
                    i = v.scalar_tensor_tensor(tmp2, tmp1, -B23, B["iota8w"][:], ALU.mult, ALU.add); VB_(i)
                    i = v.tensor_reduce(gt[:, 1, :], v3(tmp2, 8), axis=AX.X, op=ALU.min); VB_(i)
                    i = v.tensor_scalar(gt[:, 1, :], gt[:, 1, :], 1.0, B23, ALU.mult, ALU.add); VB_(i)
                    i = v.tensor_tensor(v3(tmp1, 8), v3(B["iota8w"][:], 8), bc(gt[:, 1, :], 64, 8), ALU.is_equal); VB_(i)
                    i = v.scalar_tensor_tensor(tmp2, tmp1, NEG, scb, ALU.mult, ALU.add); VB_(i)
                    i = v.reduce_max(gt[:, 2, :], v3(tmp2, 8), axis=AX.X); VB_(i)
                    i = v.tensor_add(gt[:, 3, :], gt[:, 0, :], gt[:, 2, :]); VB_(i)
                    gs = gt[:, 3, :]
                    i = v.reduce_max(st[:, 0, :], v3(gs, 4), axis=AX.X); VB_(i)
                    i = v.tensor_tensor(v3(gt[:, 4, :], 4), v3(gs, 4), bc(st[:, 0, :], 16, 4), ALU.is_equal); VB_(i)
                    i = v.scalar_tensor_tensor(gt[:, 5, :], gt[:, 4, :], -B23, B["iota4w"][:], ALU.mult, ALU.add); VB_(i)
                    i = v.tensor_reduce(st[:, 1, :], v3(gt[:, 5, :], 4), axis=AX.X, op=ALU.min); VB_(i)
                    i = v.tensor_scalar(st[:, 1, :], st[:, 1, :], 1.0, B23, ALU.mult, ALU.add); VB_(i)
                    i = v.tensor_tensor(v3(gt[:, 4, :], 4), v3(B["iota4w"][:], 4), bc(st[:, 1, :], 16, 4), ALU.is_equal); VB_(i)
                    i = v.scalar_tensor_tensor(gt[:, 5, :], gt[:, 4, :], NEG, gs, ALU.mult, ALU.add); VB_(i)
                    i = v.reduce_max(st[:, 0, :], v3(gt[:, 5, :], 4), axis=AX.X); VB_(i)
                    i = v.tensor_tensor(v3(gt[:, 6, :], 4), v3(gt[:, 5, :], 4), bc(st[:, 0, :], 16, 4), ALU.is_equal); VB_(i)
                    i = v.scalar_tensor_tensor(gt[:, 5, :], gt[:, 6, :], -B23, B["iota4w"][:], ALU.mult, ALU.add); VB_(i)
                    i = v.tensor_reduce(st[:, 1, :], v3(gt[:, 5, :], 4), axis=AX.X, op=ALU.min); VB_(i)
                    i = v.tensor_scalar(st[:, 1, :], st[:, 1, :], 1.0, B23, ALU.mult, ALU.add); VB_(i)
                    i = v.tensor_tensor(v3(gt[:, 6, :], 4), v3(B["iota4w"][:], 4), bc(st[:, 1, :], 16, 4), ALU.is_equal); VB_(i)
                    i = v.tensor_add(gt[:, 7, :], gt[:, 4, :], gt[:, 6, :]); VB_(i)
                    i = v.tensor_scalar(gt[:, 6, :], gt[:, 7, :], -NEG, NEG, ALU.mult, ALU.add); VB_(i)
                    i = v.tensor_tensor(v3(tmp1, 8), v3(scb, 8), bc(gt[:, 7, :], 64, 8), ALU.mult); VB_(i)
                    i = v.tensor_tensor(v3(msc, 8), v3(tmp1, 8), bc(gt[:, 6, :], 64, 8), ALU.add); VB_(i)
                    i = v.memset(selm, 0.0); VB_(i)
                    for k in range(6):
                        i = v.reduce_max(st[:, 0, :], v3(msc, 32), axis=AX.X); VB_(i)
                        i = v.tensor_tensor(v3(tmp1, 32), v3(msc, 32), bc(st[:, 0, :], 16, 32), ALU.is_equal); VB_(i)
                        i = v.scalar_tensor_tensor(tmp2, tmp1, -B23, B["iota32w"][:], ALU.mult, ALU.add); VB_(i)
                        i = v.tensor_reduce(st[:, 1, :], v3(tmp2, 32), axis=AX.X, op=ALU.min); VB_(i)
                        i = v.tensor_scalar(st[:, 1, :], st[:, 1, :], 1.0, B23, ALU.mult, ALU.add); VB_(i)
                        if k == 0:
                            i = v.tensor_copy(B["idx0"][:], st[:, 1, :]); VB_(i)
                        i = v.tensor_tensor(v3(tmp1, 32), v3(B["iota32w"][:], 32), bc(st[:, 1, :], 16, 32), ALU.is_equal); VB_(i)
                        i = v.tensor_add(selm, selm, tmp1); VB_(i)
                        i = v.scalar_tensor_tensor(msc, tmp1, NEG, msc, ALU.mult, ALU.add); VB_(i)
                    i = v.tensor_mul(tmp1, selm, scv); VB_(i)
                    i = v.reduce_sum(st[:, 0, :], v3(tmp1, 32), axis=AX.X); VB_(i)
                    i = v.tensor_scalar(st[:, 1, :], st[:, 0, :], 1.0, 1e-20, ALU.mult, ALU.add); VB_(i)
                    i = v.reciprocal(st[:, 0, :], st[:, 1, :]); VB_(i)
                    i = v.tensor_scalar(st[:, 1, :], st[:, 0, :], 0.1, None, ALU.mult); VB_(i)
                    i = v.tensor_tensor(v3(B["cwtm"][:], 32), v3(tmp1, 32), bc(st[:, 1, :], 16, 32), ALU.mult); VB_(i)
                    i = v.tensor_tensor(v3(tmp2, 32), v3(B["iota32w"][:], 32), bc(B["idx0"][:], 16, 32), ALU.is_equal); VB_(i)
                    t2v = tmp2.rearrange("p (a l two) -> p a l two", l=16, two=2)
                    i = v.tensor_tensor(B["acwtm"][:].rearrange("p (a l) -> p a l", l=16),
                                        t2v[:, :, :, 0], t2v[:, :, :, 1], ALU.add)
                    P.inc(i, "RT", ("RT_cw", ph))
                    P.wait(v, ("RT_cw", ph))
                pend = []
                prev_ic = [None]

                def v_stageA(i_e, c):
                    kind, e, nic = ENTRIES[i_e]
                    if prev_ic[0] is not None:
                        P.wait(v, ("VB", ph) + prev_ic[0])
                    prev_ic[0] = (i_e, c)
                    for ic in range(nic):
                        P.wait(v, ("UU", ph, i_e, c, ic))
                        P.wait(v, ("SIL", ph, i_e, c, ic))
                        ia = v.tensor_mul(B["h1"][:, ic, :], B["pU"][ic % 2][:],
                                          B["ssil"][:, ic, :])
                        P.inc(ia, "VA", ("VA", ph, i_e, c, ic))

                def v_stageB(i_e, c):
                    kind, e, nic = ENTRIES[i_e]
                    P.wait(v, ("VA", ph, i_e, c, nic - 1))
                    P.wait(v, RL.get(("hbuf", ph, i_e, c)))
                    if kind != "s":
                        P.wait(v, ("CWE", ph, i_e, c))
                    for ic in range(nic):
                        if kind == "a":
                            ib = v.scalar_tensor_tensor(B["h"][:, c % 2, ic, :],
                                                        B["h1"][:, ic, :], 0.05,
                                                        B["cwrep"][:, 0, :], ALU.mult, ALU.mult)
                        elif kind == "r":
                            ib = v.tensor_mul(B["h"][:, c % 2, ic, :], B["h1"][:, ic, :],
                                              B["cwrep"][:, 0, :])
                        else:
                            ib = v.tensor_scalar(B["h"][:, c % 2, ic, :], B["h1"][:, ic, :],
                                                 0.5, None, ALU.mult)
                    P.inc(ib, "VB", ("VB", ph, i_e, c))

                def evac(j_e, jc):
                    for dc in range(8):
                        P.wait(v, ("DD", ph, j_e, jc, dc))
                        if j_e > 0:
                            P.wait(v, ("EV", ph, j_e - 1, jc, dc))
                        ie = v.tensor_add(B["x_sb"][:, dc, jc * 512:(jc + 1) * 512],
                                          B["x_sb"][:, dc, jc * 512:(jc + 1) * 512],
                                          B["pD"][dc % 2][:])
                        P.inc(ie, "EV", ("EV", ph, j_e, jc, dc))

                if DEV_STAGE >= 3:
                    for c in range(NCH):
                        v_stageA(0, c)
                        if c >= 2:
                            evac(0, c - 2)
                        v_stageB(0, c)
                if DEV_STAGE >= 2:
                    for r in range(8):
                        P.wait(v, ("TRR", ph, r))
                        bank = B["pD"][1] if r % 2 == 0 else B["pCW2"][1]
                        if r < 4:
                            hi_dst = B["cwT"][0:32, r * 512:(r + 1) * 512]
                            i = v.tensor_copy(hi_dst, bank[0:32, 0:512])
                            P.selfbar(v, "VS", i)
                            i = v.tensor_sub(B["uscr"][0:32, 1, :], bank[0:32, 0:512], hi_dst)
                            P.selfbar(v, "VS", i)
                            i = v.tensor_copy(B["cwT"][32:64, r * 512:(r + 1) * 512], B["uscr"][0:32, 1, :])
                        else:
                            i = v.tensor_copy(B["cwT"][64:80, (r - 4) * 512:(r - 3) * 512],
                                              bank[0:16, 0:512])
                        P.inc(i, "RT", ("TRE", ph, r))
                    P.alias(("RTDONE", ph), ("TRE", ph, 7))
                if DEV_STAGE >= 3:
                    pend = [(0, 2), (0, 3)]
                    for i_e in range(1, NEXP):
                        for c in range(NCH):
                            v_stageA(i_e, c)
                            evac(*pend.pop(0))
                            v_stageB(i_e, c)
                            pend.append((i_e, c))
                    while pend:
                        evac(*pend.pop(0))
                binc(v)

            dispatch("ffn%d" % ph, em_sync, em_gpsimd, em_tensor, em_scalar, em_vector)

    def att_phase(A):
        ph = 100
        RL = plan_att_resources(ph)
        x1v = x1.rearrange("(k p) t -> p k t", p=128)
        x2v = x2.rearrange("(k p) t -> p k t", p=128)

        if True:
            def em_sync(sy):
                bwait(sy)
                P.inc(sy.dma_start(A["bvrep"][:], ap["bvrep"]), "ADC", None, 16)
                P.inc(sy.dma_start(A["bopp"][:], ap["bopp"]), "ADC", None, 16)
                i0 = sy.dma_start(A["mask"][:], ap["mask128"])
                P.inc(i0, "ADC", ("BV",), 16)
                for c in range(NCHA):
                    if c > 0:
                        P.wait(sy, ("LNA", ph, c - 1))
                    i1 = sy.dma_start(A["x1m"][:], x1v[:, :, c * 512:(c + 1) * 512])
                    P.inc(i1, "AD", ("X1C", ph, c), 16)
                P.wait(sy, ("LNA", ph, NCHA - 1))
                for m in range(NCHA):
                    P.wait(sy, RL.get(("x1m", ph, m)))
                    i2 = sy.dma_start(A["x1m"][:], x1v[:, :, m * 512:(m + 1) * 512])
                    P.inc(i2, "AD", ("X1M", ph, m), 16)
                    P.wait(sy, ("AX2", ph, m, 7))
                    i3 = sy.dma_start(x2v[:, :, m * 512:(m + 1) * 512], A["x1m"][:])
                    P.inc(i3, "AD", ("X2S", ph, m), 16)
                if not P.record:
                    sy.wait_ge(P.handles["AD"], P.events[("X2S", ph, NCHA - 1)][1])
                binc(sy)

            def em_gpsimd(g):
                bwait(g)
                P.inc(g.dma_start(A["wbT"][:].rearrange("p a b -> p (a b)"), ap["wbT"]), "AWD", None, 16)
                i1 = g.dma_start(A["wvo"][:].rearrange("p a b -> p (a b)"), ap["wvT"])
                P.inc(i1, "AWD", ("AWV",), 16)
                P.wait(g, ("VMM", ph, NCHA - 1, 3, 1))
                i2 = g.dma_start(A["wvo"][:].rearrange("p a b -> p (a b)"), ap["woT"])
                P.inc(i2, "AWD", ("AWO",), 16)
                binc(g)

            def em_tensor(t):
                bwait(t)
                P.wait(t, ("AWV",))
                for c in range(NCHA):
                    P.wait(t, RL.get(("astatA", ph, c)))
                    P.wait(t, RL.get(("astatB", ph, c)))
                    for d in range(ND):
                        P.wait(t, ("aXB", ph, c, d))
                        i1 = t.matmul(A["pA"][0][0:1, :], ones_bf[:, 0:1],
                                      A["xb"][:, d % 2, :], start=(d == 0), stop=(d == ND - 1))
                        P.inc(i1, "TS", ("aSXmm", ph, c, d))
                        P.wait(t, ("aXQ", ph, c, d))
                        i2 = t.matmul(A["pA"][1][0:1, :], ones_bf[:, 0:1],
                                      A["xq"][:, 0, :], start=(d == 0), stop=(d == ND - 1))
                        P.inc(i2, "TS", ("aSQmm", ph, c, d))
                    P.alias(("aSTAT", ph, c), ("aSQmm", ph, c, ND - 1))
                    P.wait(t, ("aHILO", ph, c))
                    P.wait(t, RL.get(("arepm", ph, c)))
                    P.wait(t, RL.get(("arepi", ph, c)))
                    t.matmul(A["pA"][0][:], negones_bf[:], A["hilo"][0:1, 0, :], start=True, stop=False)
                    t.matmul(A["pA"][0][:], negones_bf[:], A["hilo"][0:1, 1, :], start=False, stop=True)
                    t.matmul(A["pA"][1][:], ones_bf[0:1, :], A["hilo"][0:1, 2, :], start=True, stop=False)
                    i3 = t.matmul(A["pA"][1][:], ones_bf[0:1, :], A["hilo"][0:1, 3, :], start=False, stop=True)
                    P.inc(i3, "TS", ("aREP", ph, c))
                    P.wait(t, ("LNA", ph, c))
                    P.wait(t, RL.get(("lat", ph, c)))
                    for d in range(ND):
                        i4 = t.matmul(A["pA"][2][0:64, :], A["wbT"][:, d, :],
                                      A["ln2"][:, d, :], start=(d == 0), stop=(d == ND - 1))
                    P.inc(i4, "TS", ("LAT", ph, c))
                    for tt in range(4):
                        for dn in range(2):
                            P.wait(t, RL.get(("vmm", ph, c, tt, dn)))
                            bank = A["pA"][3 + (tt * 2 + dn) % 2]
                            for d in range(ND):
                                i5 = t.matmul(bank[:],
                                              A["ln2"][:, d, tt * 128:(tt + 1) * 128],
                                              A["wvo"][:, d, dn * 512:(dn + 1) * 512],
                                              start=(d == 0), stop=(d == ND - 1))
                            P.inc(i5, "TS", ("VMM", ph, c, tt, dn))
                P.wait(t, ("AWO",))
                for m in range(NCHA):
                    nkc = m + 1
                    nkt = 4 * (m + 1)
                    for qt in range(4):
                        q0 = m * 512 + qt * 128
                        for kc in range(nkc):
                            P.wait(t, RL.get(("sc", ph, m, qt, kc)))
                            i6 = t.matmul(A["pA"][kc % 2][:], A["latT"][:, q0:q0 + 128],
                                          A["latT"][:, kc * 512:(kc + 1) * 512],
                                          start=True, stop=True)
                            P.inc(i6, "TS", ("SC", ph, m, qt, kc))
                        P.wait(t, ("ANORM", ph, m, qt))
                        ngr = (nkt + 3) // 4
                        for gr in range(ngr):
                            P.wait(t, RL.get(("tr", ph, m, qt, gr)))
                            bank = A["pA"][2 + gr % 2].bitcast(BF16)
                            for k in range(min(4, nkt - gr * 4)):
                                kt = gr * 4 + k
                                i7 = t.transpose(bank[:, k * 128:(k + 1) * 128],
                                                 A["attbf"][:, kt * 128:(kt + 1) * 128],
                                                 ident_bf[:])
                            P.inc(i7, "TS", ("TR2", ph, m, qt, gr))
                        P.alias(("TRLAST", ph, m, qt), ("TR2", ph, m, qt, ngr - 1))
                    for dc in range(8):
                        P.wait(t, RL.get(("av", ph, m, dc)))
                    P.wait(t, ("ATRE", ph, m))
                    for kt in range(nkt):
                        for dc in range(8):
                            i8 = t.matmul(A["pA"][dc][:],
                                          A["v"][:, kt, dc * 128:(dc + 1) * 128],
                                          A["attT"][:, kt, :],
                                          start=(kt == 0), stop=(kt == nkt - 1))
                            if kt == nkt - 1:
                                P.inc(i8, "TS", ("AAV", ph, m, dc))
                    for dc2 in range(8):
                        P.wait(t, ("AYE", ph, m, dc2))
                    for dc2 in range(8):
                        P.wait(t, RL.get(("wo", ph, m, dc2)))
                        for d in range(ND):
                            i9 = t.matmul(A["pA"][dc2 % 2][:],
                                          A["wvo"][:, d, dc2 * 128:(dc2 + 1) * 128],
                                          A["y"][:, d, :], start=(d == 0), stop=(d == ND - 1))
                        P.inc(i9, "TS", ("AWO2", ph, m, dc2))
                    P.alias(("AWO", ph, m, 7), ("AWO2", ph, m, 7))
                binc(t)

            def em_scalar(sc_):
                bwait(sc_)
                for c in range(NCHA):
                    P.wait(sc_, ("X1C", ph, c))
                    for d in range(ND):
                        P.wait(sc_, RL.get(("axq", ph, c, d)))
                        i1 = sc_.activation(A["xq"][:, 0, :], A["x1m"][:, d, :], AF.Square)
                        P.inc(i1, "SS", ("aXQ", ph, c, d))
                    P.wait(sc_, ("aVAR", ph, c))
                    i2 = sc_.activation(A["vvec"][:], A["vvec"][:], AF.Ln)
                    P.selfbar(sc_, "SS", i2)
                    i3 = sc_.mul(A["vvec"][:], A["vvec"][:], -0.5)
                    P.selfbar(sc_, "SS", i3)
                    i4 = sc_.activation(A["vvec"][:], A["vvec"][:], AF.Exp)
                    P.inc(i4, "SS", ("aINV", ph, c))
                for m in range(NCHA):
                    nk = (m + 1) * 512
                    for qt in range(4):
                        P.wait(sc_, ("AMAX", ph, m, qt))
                        P.wait(sc_, RL.get(("attbf", ph, m, qt)))
                        i5 = sc_.activation(A["attbf"][:, 0:nk], A["scrow"][:, 0:nk],
                                            AF.Exp, bias=A["stats"][:, 0:1])
                        P.inc(i5, "SS", ("AEXP", ph, m, qt))
                binc(sc_)

            def em_vector(v):
                bwait(v)
                P.wait(v, ("BV",))
                for c in range(NCHA):
                    P.wait(v, ("X1C", ph, c))
                    for d in range(ND):
                        P.wait(v, RL.get(("axb", ph, c, d)))
                        i1 = v.tensor_copy(A["xb"][:, d % 2, :], A["x1m"][:, d, :])
                        P.inc(i1, "VS", ("aXB", ph, c, d))
                    P.wait(v, ("aSTAT", ph, c))
                    v.tensor_copy(A["sx"][0:1, 0, :], A["pA"][0][0:1, :])
                    i3 = v.tensor_copy(A["sx"][0:1, 1, :], A["pA"][1][0:1, :])
                    P.inc(i3, "VS", ("aSTEV", ph, c))
                    P.wait(v, ("aSTEV", ph, c))
                    i4 = v.tensor_scalar(A["mvec"][:], A["sx"][0:1, 0, :], 1.0 / D, None, ALU.mult)
                    P.selfbar(v, "VS", i4)
                    i5 = v.tensor_mul(A["vvec"][:], A["mvec"][:], A["mvec"][:])
                    P.selfbar(v, "VS", i5)
                    i6 = v.scalar_tensor_tensor(A["vvec"][:], A["sx"][0:1, 1, :], 1.0 / D,
                                                A["vvec"][:], ALU.mult, ALU.subtract)
                    P.selfbar(v, "VS", i6)
                    i7 = v.tensor_scalar(A["vvec"][:], A["vvec"][:], 1.0, 1e-5, ALU.mult, ALU.add)
                    P.inc(i7, "VS", ("aVAR", ph, c))
                    i8 = v.tensor_copy(A["hilo"][0:1, 0, :], A["mvec"][:])
                    P.selfbar(v, "VS", i8)
                    i9 = v.tensor_sub(A["sx"][0:1, 0, :], A["mvec"][:], A["hilo"][0:1, 0, :])
                    P.selfbar(v, "VS", i9)
                    v.tensor_copy(A["hilo"][0:1, 1, :], A["sx"][0:1, 0, :])
                    P.wait(v, ("aINV", ph, c))
                    i11 = v.tensor_copy(A["hilo"][0:1, 2, :], A["vvec"][:])
                    P.selfbar(v, "VS", i11)
                    i12 = v.tensor_sub(A["sx"][0:1, 1, :], A["vvec"][:], A["hilo"][0:1, 2, :])
                    P.selfbar(v, "VS", i12)
                    i13 = v.tensor_copy(A["hilo"][0:1, 3, :], A["sx"][0:1, 1, :])
                    P.inc(i13, "VS", ("aHILO", ph, c))
                    P.wait(v, ("aREP", ph, c))
                    P.wait(v, RL.get(("ln2", ph, c)))
                    for d in range(ND):
                        i14 = v.tensor_add(A["x1m"][:, d, :], A["x1m"][:, d, :], A["pA"][0][:])
                    P.selfbar(v, "VS", i14)
                    for d in range(ND):
                        i15 = v.tensor_mul(A["ln2"][:, d, :], A["x1m"][:, d, :], A["pA"][1][:])
                    P.inc(i15, "VS", ("LNA", ph, c))
                    P.wait(v, ("LAT", ph, c))
                    i16 = v.tensor_copy(A["latT"][0:64, c * 512:(c + 1) * 512], A["pA"][2][0:64, :])
                    P.inc(i16, "VS", ("LATE", ph, c))
                    for tt in range(4):
                        for dn in range(2):
                            P.wait(v, ("VMM", ph, c, tt, dn))
                            bank = A["pA"][3 + (tt * 2 + dn) % 2]
                            i17 = v.tensor_add(A["v"][:, c * 4 + tt, dn * 512:(dn + 1) * 512],
                                               bank[:], A["bvrep"][:, dn * 512:(dn + 1) * 512])
                            P.inc(i17, "VS", ("VE", ph, c, tt, dn))
                for m in range(NCHA):
                    nkc = m + 1
                    nkt = 4 * (m + 1)
                    nk = nkc * 512
                    for qt in range(4):
                        q0 = m * 512 + qt * 128
                        P.wait(v, RL.get(("scrow", ph, m, qt)))
                        for kc in range(nkc):
                            P.wait(v, ("SC", ph, m, qt, kc))
                            i1 = v.tensor_scalar(A["scrow"][:, kc * 512:(kc + 1) * 512],
                                                 A["pA"][kc % 2][:], 0.125, None, ALU.mult)
                            P.inc(i1, "VS", ("SCE", ph, m, qt, kc))
                        P.wait(v, ("SCE", ph, m, qt, nkc - 1))
                        i2 = v.tensor_add(A["scrow"][:, q0:q0 + 128],
                                          A["scrow"][:, q0:q0 + 128], A["mask"][:])
                        if q0 + 128 < nk:
                            i2 = v.memset(A["scrow"][:, q0 + 128:nk], NEG)
                        P.selfbar(v, "VS", i2)
                        i3 = v.reduce_max(A["stats"][:, 0:1], A["scrow"][:, 0:nk],
                                          axis=AX.X, negate=True)
                        P.inc(i3, "VS", ("AMAX", ph, m, qt))
                        P.wait(v, ("AEXP", ph, m, qt))
                        i4 = v.reduce_sum(A["stats"][:, 1:2], A["attbf"][:, 0:nk], axis=AX.X)
                        P.selfbar(v, "VS", i4)
                        i5 = v.reciprocal(A["stats"][:, 2:3], A["stats"][:, 1:2])
                        P.selfbar(v, "VS", i5)
                        i6 = v.tensor_scalar(A["attbf"][:, 0:nk], A["attbf"][:, 0:nk],
                                             A["stats"][:, 2:3], None, ALU.mult)
                        P.inc(i6, "VS", ("ANORM", ph, m, qt))
                        if qt == 0:
                            P.wait(v, RL.get(("attT", ph, m)))
                        ngr = (nkt + 3) // 4
                        for gr in range(ngr):
                            P.wait(v, ("TR2", ph, m, qt, gr))
                            bank = A["pA"][2 + gr % 2].bitcast(BF16)
                            nkk = min(4, nkt - gr * 4)
                            i7 = v.tensor_copy(A["attT"][:, gr * 4:gr * 4 + nkk, qt * 128:(qt + 1) * 128],
                                               bank.rearrange("p (k f) -> p k f", f=128)[:, 0:nkk, :])
                            P.inc(i7, "VS", ("TRE2", ph, m, qt, gr))
                    P.alias(("ATRE", ph, m), ("TRE2", ph, m, 3, ((nkt + 3) // 4) - 1))
                    P.wait(v, RL.get(("ysb", ph, m)))
                    for dc in range(8):
                        P.wait(v, ("AAV", ph, m, dc))
                        i8 = v.tensor_copy(A["y"][:, dc, :], A["pA"][dc][:])
                        P.inc(i8, "VS", ("AYE", ph, m, dc))
                    P.wait(v, ("X1M", ph, m))
                    for dc2 in range(8):
                        P.wait(v, ("AWO2", ph, m, dc2))
                        P.wait(v, RL.get(("osb", ph, m, dc2)))
                        i8b = v.tensor_scalar(A["osb"][:, dc2 % 2, :], A["pA"][dc2 % 2][:],
                                              A["bopp"][:, dc2:dc2 + 1], None, ALU.add)
                        P.inc(i8b, "VS", ("ACP", ph, m, dc2))
                        P.wait(v, ("ACP", ph, m, dc2))
                        i9 = v.tensor_add(A["x1m"][:, dc2, :], A["x1m"][:, dc2, :],
                                          A["osb"][:, dc2 % 2, :])
                        P.inc(i9, "VS", ("AX2", ph, m, dc2))
                binc(v)

            dispatch("att", em_sync, em_gpsimd, em_tensor, em_scalar, em_vector)

    _arena_ctr = [0]

    def ffn_arena(fes):
        _arena_ctr[0] += 1
        tag = "ffn%d_" % _arena_ctr[0]

        def fsb(name, shape, dt):
            return fes.enter_context(nc.sbuf_tensor(tag + name, list(shape), dt))
        return dict(
            x_sb=fsb("x", (128, ND, SEG), F32),
            ln_sb=fsb("ln", (128, ND, SEG), BF16),
            wb=[fsb("wb%d" % p, (128, 12288), BF16) for p in range(2)],
            h1=fsb("h1", (128, 4, 512), BF16),
            ssil=fsb("ssil", (128, 4, 512), BF16),
            h=fsb("h", (128, 2, 4, 512), BF16),
            xb=fsb("xb", (128, 2, 512), BF16),
            xq=fsb("xq", (128, 1, 512), BF16),
            cwrep=fsb("cwrep", (128, 1, 512), F32),
            uscr=fsb("uscr", (128, 4, 512), F32),
            sc=fsb("scw", (128, 512), F32),
            cwT=fsb("cwT", (128, SEG), BF16),
            sx=fsb("sx", (1, 2, 512), F32),
            mvec=fsb("mvec", (1, 512), F32),
            vvec=fsb("vvec", (1, 512), F32),
            hilo=fsb("hilo", (1, 4, 512), BF16),
            st=fsb("st", (128, 2, 16), F32),
            gt=fsb("gt", (128, 8, 64), F32),
            idx0=fsb("idx0", (128, 16), F32),
            ohsel32=fsb("oh32", (128, 32 * 128), BF16),
            iota32w=fsb("i32", (128, 512), BF16),
            iota8w=fsb("i8", (128, 512), BF16),
            iota4w=fsb("i4", (128, 64), BF16),
            gbw=fsb("gbw", (128, 512), BF16),
            pG=[banks[0], banks[1]],
            pU=[banks[2], banks[3]],
            pD=[banks[4], banks[5]],
            pCW2=[banks[6], banks[7]],
        )

    def emit_all():
        bar["target"] = 0
        nphase = 0
        first = True
        if "f1" in phases:
            with ExitStack() as fes:
                FB = ffn_arena(fes)
                for s_ in range(NSEG):
                    ffn_seg(0 + s_, s_, ap["xT"], x1, FB, first)
                    first = False
                    nphase += 1
                    bar["target"] = 5 * nphase
        if "att" in phases:
            with ExitStack() as aes:
                _arena_ctr[0] += 1
                _atag = "att%d_" % _arena_ctr[0]

                def asb(name, shape, dt):
                    return aes.enter_context(nc.sbuf_tensor(_atag + name, list(shape), dt))
                AB = dict(
                    v=asb("v", (128, 32, 1024), BF16),
                    attT=asb("attT", (128, 32, 512), BF16),
                    latT=asb("latT", (64, T), BF16),
                    ln2=asb("ln2", (128, ND, 512), BF16),
                    x1m=asb("x1m", (128, ND, 512), F32),
                    wvo=asb("wvo", (128, ND, 1024), BF16),
                    wbT=asb("wbT", (128, ND, 64), BF16),
                    bvrep=asb("bvrep", (128, D), F32),
                    bopp=asb("bopp", (128, 8), F32),
                    mask=asb("mask", (128, 128), F32),
                    scrow=asb("scrow", (128, T), F32),
                    attbf=asb("attbf", (128, T), BF16),
                    y=asb("y", (128, 8, 512), BF16),
                    osb=asb("o", (128, 2, 512), F32),
                    stats=asb("stats", (128, 4), F32),
                    xb=asb("xb", (128, 2, 512), BF16),
                    xq=asb("xq", (128, 1, 512), BF16),
                    sx=asb("sx", (1, 2, 512), F32),
                    mvec=asb("mvec", (1, 512), F32),
                    vvec=asb("vvec", (1, 512), F32),
                    hilo=asb("hilo", (1, 4, 512), BF16),
                    pA=banks,
                )
                att_phase(AB)
            nphase += 1
            bar["target"] = 5 * nphase
        if "f2" in phases:
            with ExitStack() as fes:
                FB = ffn_arena(fes)
                for s_ in range(NSEG):
                    ffn_seg(200 + s_, s_, x2, outT, FB, first)
                    first = False
                    nphase += 1
                    bar["target"] = 5 * nphase

    P.start_pass(True)
    emit_all()
    P.start_pass(False)
    emit_all()
    es.close()
    return nc


def _consts():
    c = {}
    c["ones128"] = np.ones((128, 128), dtype=np.float32)
    c["negones"] = -np.ones((1, 128), dtype=np.float32)
    c["ident"] = np.eye(128, dtype=np.float32)
    oh32 = np.zeros((128, 32 * 128), dtype=np.float32)
    for e in range(32):
        oh32[e, e * 128:(e + 1) * 128] = 1.0
        oh32[32 + e, e * 128:(e + 1) * 128] = 1.0
    for l in range(16):
        oh32[64 + l, l * 128:(l + 1) * 128] = 1.0
    c["ohsel32"] = oh32
    i32 = np.tile(np.arange(32, dtype=np.float32), 16)
    c["iota32w"] = np.ascontiguousarray(np.tile(i32, (128, 1)))
    i8 = np.tile(np.arange(8, dtype=np.float32), 64)
    c["iota8w"] = np.ascontiguousarray(np.tile(i8, (128, 1)))
    i4 = np.tile(np.arange(4, dtype=np.float32), 16)
    c["iota4w"] = np.ascontiguousarray(np.tile(i4, (128, 1)))
    m = np.zeros((128, 128), dtype=np.float32)
    for p_ in range(128):
        m[p_, p_ + 1:] = NEG
    c["mask128"] = m
    return c


def make_in_maps(inputs):
    x = np.asarray(inputs["L0_attn_hidden_states"], dtype=np.float32)
    gb = np.asarray(inputs["L0_ffn_gate_bias"], dtype=np.float32)
    consts = _consts()
    def fm(w, isz):
        return w.reshape(ND, 128, isz).transpose(1, 0, 2).reshape(128, ND * isz)

    def im(w, nic):
        return w.reshape(nic, 128, D).transpose(1, 0, 2).reshape(128, nic * D)

    wgf = np.asarray(inputs["L0_ffn_w_gate"], np.float32)
    wuf = np.asarray(inputs["L0_ffn_w_up"], np.float32)
    wdf = np.asarray(inputs["L0_ffn_w_down"], np.float32)
    awgf = np.asarray(inputs["L0_ffn_anchor_w_gate"], np.float32)
    awuf = np.asarray(inputs["L0_ffn_anchor_w_up"], np.float32)
    awdf = np.asarray(inputs["L0_ffn_anchor_w_down"], np.float32)
    wall_r = np.empty((NE + 1, 128, 12288), dtype=np.float32)
    for e in range(NE):
        wall_r[e, :, 0:4096] = fm(wgf[e], 512)
        wall_r[e, :, 4096:8192] = fm(wuf[e], 512)
        wall_r[e, :, 8192:12288] = im(wdf[e], 4)
    wall_r[NE, :, 0:4096] = fm(np.asarray(inputs["L0_ffn_shared_w_gate"], np.float32), 512)
    wall_r[NE, :, 4096:8192] = fm(np.asarray(inputs["L0_ffn_shared_w_up"], np.float32), 512)
    wall_r[NE, :, 8192:12288] = im(np.asarray(inputs["L0_ffn_shared_w_down"], np.float32), 4)
    wall_a = np.empty((NL, 128, 6144), dtype=np.float32)
    for l in range(NL):
        wall_a[l, :, 0:2048] = fm(awgf[l], 256)
        wall_a[l, :, 2048:4096] = fm(awuf[l], 256)
        wall_a[l, :, 4096:6144] = im(awdf[l], 2)

    shared = dict(
        wbT=np.ascontiguousarray(fm(np.asarray(inputs["L0_attn_W_basis"], np.float32).T, 64)),
        wvT=np.ascontiguousarray(fm(np.asarray(inputs["L0_attn_W_v"], np.float32).T, D)),
        woT=np.ascontiguousarray(fm(np.asarray(inputs["L0_attn_W_o"], np.float32).T, D)),
        bvrep=np.ascontiguousarray(np.tile(np.asarray(inputs["L0_attn_b_v"], np.float32)[None, :], (128, 1))),
        bopp=np.ascontiguousarray(np.asarray(inputs["L0_attn_b_o"], np.float32).reshape(8, 128).T),
        gwT=np.ascontiguousarray(np.asarray(inputs["L0_ffn_gate_weight"], np.float32).T),
        gbw=np.ascontiguousarray(np.tile(np.tile(gb, 16)[None, :], (128, 1))),
        wall_r=wall_r,
        wall_a=wall_a,
        **consts,
    )
    in_maps = []
    for b in range(8):
        m = dict(shared)
        m["xT"] = np.ascontiguousarray(x[b].T)
        in_maps.append(m)
    return in_maps


def kernel(**inputs):
    in_maps = make_in_maps(inputs)
    nc = build_program()
    res = run_bass_kernel_spmd(nc, in_maps, core_ids=list(range(8)))
    out = np.stack([np.asarray(r["outT"]).T for r in res.results], axis=0)
    return out.astype(np.float32)
